# version 24
# baseline (speedup 1.0000x reference)
import numpy as np
import concourse.bass as bass
import concourse.mybir as mybir
from concourse.bass_utils import run_bass_kernel_spmd
from concourse.ap import AP

F32 = mybir.dt.float32
BF16 = mybir.dt.bfloat16
AF = mybir.ActivationFunctionType
ALU = mybir.AluOpType
AX = mybir.AxisListType

D = 1024
DC = 8
H = 16
HD = 64
NE = 8
NR = 16
THRESH = 0.29514
SHARP = 15.0
EPS = 1e-5

SEMS = (["T", "V", "S", "G"] + [f"dm{i}" for i in range(8)]
        + [f"dn{i}" for i in range(4)]
        + [f"dg{i}" for i in range(80)])


class Ctx:
    def __init__(self, nc, record, tbl):
        self.nc = nc
        self.record = record
        self.tbl = tbl
        self.sem = {n: nc.alloc_semaphore(n) for n in SEMS}
        self.cnt = {n: 0 for n in SEMS}
        self.dmai = 0
        self.dmni = 0
        self.dmgi = 0

    def inc(self, instr, s, n=1, mark=None):
        if s in ("dW", "dD"):
            s = f"dg{self.dmgi}"
            self.dmgi += 1
            n = 16
        elif s.startswith("d"):
            s = f"dm{self.dmai % 8}"
            self.dmai += 1
            n = 16
        self.cnt[s] += n
        instr.then_inc(self.sem[s], n)
        if mark is not None and self.record:
            assert mark not in self.tbl, mark
            self.tbl[mark] = (s, self.cnt[s])
        return instr

    def wait(self, eng, mark):
        if mark is None or self.record:
            return
        s, v = self.tbl[mark]
        eng.wait_ge(self.sem[s], v)

    def dma(self, eng, fn, pool, mark=None):
        if pool == "g":
            s = f"dg{self.dmgi}"
            self.dmgi += 1
        elif pool == "n":
            s = f"dn{self.dmni % 4}"
            self.dmni += 1
        else:
            s = f"dm{self.dmai % 8}"
            self.dmai += 1
        prevc = self.cnt[s]
        if prevc and not self.record:
            eng.wait_ge(self.sem[s], prevc)
        self.cnt[s] += 16
        instr = fn()
        instr.then_inc(self.sem[s], 16)
        if mark is not None and self.record:
            assert mark not in self.tbl, mark
            self.tbl[mark] = (s, self.cnt[s])
        return instr


def _emit(nc, c, TC, dbg):
    T = TC * 128
    HP = H // 2

    dram = {"x": nc.dram_tensor("x", [T, D], F32, kind="ExternalInput")}
    for nm, shp in [("aWq", [D, D]), ("aWk", [D, D]), ("aWv", [D, D]),
                    ("abq", [D]), ("abk", [D]), ("abv", [D]),
                    ("bWq", [D, D]), ("bWk", [D, D]), ("bWv", [D, D]),
                    ("bWo", [D, D]), ("bgq", [HD]), ("bgk", [HD]),
                    ("bbo", [D]), ("blnw", [D]), ("blnb", [D]),
                    ("fWg", [D, NE]), ("fA", [NE, D, NR]), ("fB", [NE, NR, D])]:
        dram[nm] = nc.dram_tensor(nm, shp, F32, kind="ExternalInput")
    dy = nc.dram_tensor("y", [T, D], F32, kind="ExternalOutput")
    dskew = nc.dram_tensor("skew", [DC * 128 * 2177], BF16, kind="Internal")
    ds3 = nc.dram_tensor("s3", [3 * D], BF16, kind="Internal")
    dxb0 = nc.dram_tensor("xb0", [T, D], F32, kind="Internal")
    dxb1 = nc.dram_tensor("xb1", [T, D], F32, kind="Internal")
    dtap = {}
    if dbg:
        for nm in ("t_xa", "t_xb", "t_xc"):
            dtap[nm] = nc.dram_tensor(nm, [T, D], F32, kind="ExternalOutput")

    def xrows(dt_, tc):
        return dt_.ap()[128 * tc:128 * (tc + 1), :]

    ones_row = nc.alloc_sbuf_tensor([1, 512], BF16)
    ones_col = nc.alloc_sbuf_tensor([128, 1], BF16)
    ident = nc.alloc_sbuf_tensor([128, 128], BF16)
    E2rep = nc.alloc_sbuf_tensor([128, 128], BF16)
    gq2rep = nc.alloc_sbuf_tensor([128, 128], BF16)
    E2C = nc.alloc_sbuf_tensor([128, 2], F32)
    gq2c = nc.alloc_sbuf_tensor([128, 2], BF16)
    gk2 = nc.alloc_sbuf_tensor([128, 2], BF16)
    gcol = nc.alloc_sbuf_tensor([128, 2], F32)
    E8 = nc.alloc_sbuf_tensor([8, 128], BF16)
    cpool = nc.alloc_sbuf_tensor([128, 4], F32)
    mv1 = nc.alloc_sbuf_tensor([128, 2 * TC], F32)
    mv2 = nc.alloc_sbuf_tensor([128, 2 * TC], F32)
    inv1 = nc.alloc_sbuf_tensor([128, TC], F32)
    inv2 = nc.alloc_sbuf_tensor([128, TC], F32)
    st6 = nc.alloc_sbuf_tensor([128, 12], F32)
    nsc = nc.alloc_sbuf_tensor([128, 2 * TC], F32)
    cwb = nc.alloc_sbuf_tensor([128, NE * TC], BF16)
    scf1 = nc.alloc_sbuf_tensor([128, 1024], F32)
    scb1 = nc.alloc_sbuf_tensor([128, 1024], BF16)
    xcb = nc.alloc_sbuf_tensor([128, 2 * 1024], F32)
    xcb2 = nc.alloc_sbuf_tensor([128, 2 * 1024], F32)
    brow = nc.alloc_sbuf_tensor([1, D], BF16)
    Wb0 = nc.alloc_sbuf_tensor([128, DC * 1024], BF16)
    fmA = nc.alloc_sbuf_tensor([128, DC * T], BF16)

    _closers = {}

    def salloc(key, shape, dt_, side="left"):
        g_ = nc.sbuf_tensor(shape, dt_, side=side)
        h = g_.__enter__()
        _closers[key] = g_
        return h

    def sfree(key):
        _closers.pop(key).__exit__(None, None, None)

    pA = nc.alloc_psum_tensor([128, 1024], F32)
    pB = nc.alloc_psum_tensor([128, 1024], F32)
    pP1 = nc.alloc_psum_tensor([128, 1024], F32)
    pP2 = nc.alloc_psum_tensor([128, 1024], F32)
    pTb = AP(pP2.ap().tensor, 0, [[1024, 128], [1, 1024]]).bitcast(BF16)
    gl = xcb[:, 0:NE * TC]
    gsh = xcb[:, NE * TC:2 * NE * TC]
    gih = xcb[:, 2 * NE * TC:3 * NE * TC]
    gm = xcb[:, 3 * NE * TC:3 * NE * TC + 6 * TC]

    work = {"t": [], "v": [], "s": [], "g": [], "y": []}

    def op(e):
        def deco(fn):
            work[e].append(fn)
        return deco

    seq = {"n": 0}

    def mk(tag):
        seq["n"] += 1
        return f"{tag}.{seq['n']}"

    last = {"pA": None, "pB": None, "pP1": None, "pP2": None, "pTb": None,
            "scb1g": None}

    @op("v")
    def _(v):
        v.memset(ones_row[:], 1.0)
        v.memset(ones_col[:], 1.0)
        v.memset(cpool[:, 0:1], -THRESH * SHARP)
        c.inc(v.memset(cpool[:, 1:2], SHARP), "V", mark="set:v")

    @op("g")
    def _(g):
        def fence(instr, m):
            c.inc(instr, "G", mark=m)
            c.wait(g, m)
        fence(g.memset(ident[:], 0.0), "id:ms")
        fence(g.affine_select(out=ident[:], in_=ident[:],
                              compare_op=ALU.not_equal, fill=1.0, base=0,
                              pattern=[[-1, 128]], channel_multiplier=1),
              "id:done")
        s0, s1, s2 = scf1[:, 0:128], scf1[:, 128:256], scf1[:, 256:384]
        fence(g.iota(s0, pattern=[[-64, 2], [0, 64]], base=0,
                     channel_multiplier=1, allow_small_or_imprecise_dtypes=True), "e2:i")
        fence(g.tensor_scalar(s1, s0, 0.0, None, ALU.is_ge), "e2:a")
        fence(g.tensor_scalar(s2, s0, 64.0, None, ALU.is_lt), "e2:b")
        fence(g.tensor_tensor(E2rep[:], s1, s2, ALU.mult), "e2:done")
        fence(g.iota(s0[0:8, :], pattern=[[1, 128]], base=0,
                     channel_multiplier=-16, allow_small_or_imprecise_dtypes=True), "e8:i")
        fence(g.tensor_scalar(s1[0:8, :], s0[0:8, :], 0.0, None,
                              ALU.is_ge), "e8:a")
        fence(g.tensor_scalar(s2[0:8, :], s0[0:8, :], 16.0, None,
                              ALU.is_lt), "e8:b")
        fence(g.tensor_tensor(E8[:], s1[0:8, :], s2[0:8, :], ALU.mult),
              "e8:done")
        fence(g.iota(s0[:, 0:2], pattern=[[-64, 2]], base=0,
                     channel_multiplier=1, allow_small_or_imprecise_dtypes=True), "e2c:i")
        fence(g.tensor_scalar(s1[:, 0:2], s0[:, 0:2], 0.0, None, ALU.is_ge),
              "e2c:a")
        fence(g.tensor_scalar(s2[:, 0:2], s0[:, 0:2], 64.0, None, ALU.is_lt),
              "e2c:b")
        fence(g.tensor_tensor(E2C[:], s1[:, 0:2], s2[:, 0:2], ALU.mult),
              "e2c:done")
        c.dma(g, lambda: g.dma_start(out=gcol[0:64, 0:1],
                          in_=dram["bgq"].ap()[:, None]), "g", mark="g:q0")
        c.dma(g, lambda: g.dma_start(out=gcol[64:128, 0:1],
                          in_=dram["bgq"].ap()[:, None]), "g", mark="g:q1")
        c.dma(g, lambda: g.dma_start(out=gcol[0:64, 1:2],
                          in_=dram["bgk"].ap()[:, None]), "g", mark="g:k0")
        c.dma(g, lambda: g.dma_start(out=gcol[64:128, 1:2],
                          in_=dram["bgk"].ap()[:, None]), "g", mark="g:k1")
        c.wait(g, "g:q0")
        c.wait(g, "g:q1")
        fence(g.tensor_scalar(gq2c[:], E2C[:], gcol[:, 0:1], None, ALU.mult),
              "gq2c")
        fence(g.tensor_scalar(gq2rep[:], E2rep[:], gcol[:, 0:1], None,
                              ALU.mult), "gq2rep")
        c.wait(g, "g:k0")
        c.wait(g, "g:k1")
        fence(g.tensor_scalar(gk2[:], E2C[:], gcol[:, 1:2], None, ALU.mult),
              "setup:g")

    def wload(dst, name, m, waits=()):
        @op("g")
        def _(g):
            for w in waits:
                c.wait(g, w)
            c.dma(g, lambda: g.dma_start(
                out=dst.ap().rearrange("p (ch e) -> p ch e", ch=DC),
                in_=AP(dram[name], 0,
                       [[1024, 128], [128 * 1024, DC], [1, 1024]])), "g", mark=m)

    def bload(slot, name, m, waits=()):
        @op("g")
        def _(g):
            for w in waits:
                c.wait(g, w)
            c.dma(g, lambda: g.dma_start(out=brow[0:1, slot * D:(slot + 1) * D],
                              in_=dram[name].ap()[None, :]), "g", mark=m)

    def batch_inv(tag, mvbuf, invbuf, readymark):
        m_eps, m_sq, m_inv = mk(tag + "E"), mk(tag + "Q"), mk(tag + "I")

        @op("v")
        def _(v):
            c.wait(v, readymark)
            v.drain()
            varap = AP(mvbuf.ap().tensor, mvbuf.ap().offset + 1,
                       [[mvbuf.ap().ap[0][0], 128], [2, TC]])
            c.inc(v.tensor_scalar(nsc[:, 0:TC], varap, EPS, None, ALU.add),
                  "V", mark=m_eps)

        @op("s")
        def _(s):
            c.wait(s, m_eps)
            c.inc(s.activation(nsc[:, TC:2 * TC], nsc[:, 0:TC], AF.Sqrt),
                  "S", mark=m_sq)

        @op("v")
        def _(v):
            c.wait(v, m_sq)
            v.drain()
            c.inc(v.reciprocal(invbuf[:, 0:TC], nsc[:, TC:2 * TC]), "V",
                  mark=m_inv)
        return m_inv

    ln_guard = {"m": None}

    def stream_stats(tag, src, mvbuf, invbuf, src_marks):
        ap_marks = []
        lg = ln_guard["m"]
        for tc in range(TC):
            lm, st = mk(tag + "L"), mk(tag + "S")
            ap_marks.append(st)

            @op("y")
            def _(y, tc=tc, lm=lm, lg=lg):
                if src_marks is not None:
                    c.wait(y, src_marks[tc])
                if tc >= 2:
                    c.wait(y, ap_marks[tc - 2])
                elif lg is not None:
                    c.wait(y, lg)
                c.dma(y, lambda: y.dma_start(
                    out=xcb2[:, 1024 * (tc % 2):1024 * (tc % 2) + 1024],
                    in_=xrows(src, tc)), "m", mark=lm)

            @op("v")
            def _(v, tc=tc, lm=lm, st=st):
                c.wait(v, lm)
                v.drain()
                xv = xcb2[:, 1024 * (tc % 2):1024 * (tc % 2) + 1024]
                v.bn_stats(st6[:, 0:6], xv[:, 0:512])
                v.bn_stats(st6[:, 6:12], xv[:, 512:1024])
                v.drain()
                c.inc(v.bn_aggr(mvbuf[:, 2 * tc:2 * tc + 2], st6[:, 0:12]),
                      "V", mark=st)
        ln_guard["m"] = ap_marks[TC - 1]
        return batch_inv(tag, mvbuf, invbuf, ap_marks[TC - 1])

    tp_ping = {"i": 0, "prev": [None, None]}

    def transposes(tag, tc, srcmark, src_bf, dst_fm, wguard, tstride=None,
                   toff=None, marks=None):
        if tstride is None:
            tstride = T
        if toff is None:
            toff = 128 * tc
        if marks is None:
            marks = (mk(tag + "TM"), mk(tag + "TE"))
        m_mm, m_ev = marks
        half = tp_ping["i"] % 2
        tp_ping["i"] += 1
        prev = tp_ping["prev"][half]
        tp_ping["prev"][half] = m_ev
        last["pTb"] = m_ev
        base = 1024 * half

        @op("t")
        def _(t):
            c.wait(t, srcmark)
            c.wait(t, "id:done")
            c.wait(t, prev)
            for j in range(8):
                mmi = t.transpose(pTb[:, base + 128 * j:base + 128 * (j + 1)],
                                  src_bf[:, 128 * j:128 * (j + 1)], ident[:])
            c.inc(mmi, "T", mark=m_mm)

        @op("v")
        def _(v):
            c.wait(v, m_mm)
            c.wait(v, wguard)
            dap = AP(dst_fm.ap().tensor, dst_fm.ap().offset + toff,
                     [[dst_fm.ap().ap[0][0], 128], [tstride, 8], [1, 128]])
            c.inc(v.tensor_copy(
                dap,
                pTb[:, base:base + 1024].rearrange("p (j q) -> p j q", j=8)),
                "V", mark=m_ev)
        return m_mm, m_ev

    def ln_fused(tag, src, mvbuf, invbuf, dst_fm, fm_guard, src_marks=None,
                 inner=False, lazy=False, stage=None, stage_from=0,
                 stage_marks=None):
        ld_m = [mk(tag + "L") for _ in range(TC)]
        st_m = [mk(tag + "S") for _ in range(TC)]
        qq_m = [mk(tag + "Q") for _ in range(TC)]
        ap_m = [mk(tag + "A") for _ in range(TC)]
        tp_m = [(mk(tag + "TM"), mk(tag + "TE")) for _ in range(TC)]
        s2_m = [mk(tag + "T") for _ in range(TC)] if inner else []
        q2_m = [mk(tag + "U") for _ in range(TC)] if inner else []
        i2_m = [mk(tag + "I") for _ in range(TC)] if inner else []
        ev_list = [m[1] for m in tp_m]
        lg = ln_guard["m"]
        ln_guard["m"] = ap_m[TC - 1]
        g0 = last["scb1g"]
        nctr = [0]

        def emit_chunk():
            tc = nctr[0]
            nctr[0] += 1
            lm, m_st, m_qq, ma = ld_m[tc], st_m[tc], qq_m[tc], ap_m[tc]
            prev_mm = tp_m[tc - 1][0] if tc else None

            staged = stage is not None and tc >= stage_from
            if not staged:
                @op("y")
                def _(y, tc=tc, lm=lm):
                    if src_marks is not None:
                        c.wait(y, src_marks[tc])
                    if tc >= 2:
                        c.wait(y, ap_m[tc - 2])
                    else:
                        c.wait(y, lg)
                    c.dma(y, lambda: y.dma_start(
                        out=xcb2[:, 1024 * (tc % 2):1024 * (tc % 2) + 1024],
                        in_=xrows(src, tc)), "m", mark=lm)

            def xsrc(tc, staged=staged):
                if not staged:
                    return xcb2[:, 1024 * (tc % 2):1024 * (tc % 2) + 1024]
                return stage[tc % len(stage)]

            @op("v")
            def _(v, tc=tc, lm=lm, m_st=m_st, staged=staged):
                if not staged:
                    c.wait(v, lm)
                else:
                    c.wait(v, (stage_marks or src_marks)[tc])
                v.drain()
                xv = xsrc(tc)
                v.bn_stats(st6[:, 0:6], xv[:, 0:512])
                v.bn_stats(st6[:, 6:12], xv[:, 512:1024])
                v.drain()
                v.bn_aggr(mvbuf[:, 2 * tc:2 * tc + 2], st6[:, 0:12])
                v.drain()
                c.inc(v.tensor_scalar(nsc[:, tc:tc + 1],
                                      mvbuf[:, 2 * tc + 1:2 * tc + 2],
                                      EPS, None, ALU.add), "V", mark=m_st)

            @op("s")
            def _(s, tc=tc, m_st=m_st, m_qq=m_qq):
                c.wait(s, m_st)
                c.inc(s.activation(nsc[:, TC + tc:TC + tc + 1],
                                   nsc[:, tc:tc + 1], AF.Sqrt), "S",
                      mark=m_qq)

            if not inner:
                @op("v")
                def _(v, tc=tc, m_qq=m_qq, ma=ma, prev_mm=prev_mm,
                      g0=(g0 if tc == 0 else None)):
                    c.wait(v, m_qq)
                    c.wait(v, g0)
                    c.wait(v, prev_mm)
                    v.reciprocal(invbuf[:, tc:tc + 1],
                                 nsc[:, TC + tc:TC + tc + 1])
                    v.drain()
                    xv = xsrc(tc)
                    c.inc(v.tensor_scalar(scb1[:], xv,
                                          mvbuf[:, 2 * tc:2 * tc + 1],
                                          invbuf[:, tc:tc + 1],
                                          ALU.subtract, ALU.mult), "V",
                          mark=ma)
            else:
                m_s2, m_q2, m_i2 = s2_m[tc], q2_m[tc], i2_m[tc]

                @op("v")
                def _(v, tc=tc, m_qq=m_qq, m_s2=m_s2,
                      g0=(g0 if tc == 0 else None)):
                    c.wait(v, m_qq)
                    c.wait(v, g0)
                    v.reciprocal(invbuf[:, tc:tc + 1],
                                 nsc[:, TC + tc:TC + tc + 1])
                    v.drain()
                    xv = xsrc(tc)
                    v.tensor_scalar(scf1[:], xv,
                                    mvbuf[:, 2 * tc:2 * tc + 1],
                                    invbuf[:, tc:tc + 1], ALU.subtract,
                                    ALU.mult)
                    v.drain()
                    v.bn_stats(st6[:, 0:6], scf1[:, 0:512])
                    v.bn_stats(st6[:, 6:12], scf1[:, 512:1024])
                    v.drain()
                    v.bn_aggr(mv2[:, 2 * tc:2 * tc + 2], st6[:, 0:12])
                    v.drain()
                    c.inc(v.tensor_scalar(nsc[:, tc:tc + 1],
                                          mv2[:, 2 * tc + 1:2 * tc + 2],
                                          EPS, None, ALU.add), "V",
                          mark=m_s2)

                @op("s")
                def _(s, tc=tc, m_s2=m_s2, m_q2=m_q2):
                    c.wait(s, m_s2)
                    c.inc(s.activation(nsc[:, TC + tc:TC + tc + 1],
                                       nsc[:, tc:tc + 1], AF.Sqrt), "S",
                          mark=m_q2)

                @op("v")
                def _(v, tc=tc, m_q2=m_q2, m_i2=m_i2, ma=ma,
                      prev_mm=prev_mm):
                    c.wait(v, m_q2)
                    c.wait(v, prev_mm)
                    c.inc(v.reciprocal(inv2[:, tc:tc + 1],
                                       nsc[:, TC + tc:TC + tc + 1]), "V",
                          mark=m_i2)
                    v.drain()
                    c.inc(v.tensor_copy(scb1[:], scf1[:]), "V", mark=ma)

            fg = fm_guard["m"] if isinstance(fm_guard, dict) else fm_guard
            transposes(tag, tc, ma, scb1, dst_fm,
                       fg if tc == 0 else None, marks=tp_m[tc])
            last["scb1g"] = tp_m[tc][0]

        def flush():
            while nctr[0] < TC:
                emit_chunk()

        obj = {"ev": ev_list, "i2": i2_m, "emit": emit_chunk,
               "flush": flush, "ap": ap_m}
        if not lazy:
            flush()
        if inner:
            return obj if lazy else (ev_list, i2_m)
        return obj if lazy else ev_list

    def ln_pass(tag, src, mvbuf, invbuf, statsmark, dst_fm, fm_guard,
                inner=False, src_marks=None):
        ap_marks = []
        ev_list = []
        last_ev = None
        prev_mm = None
        lg = ln_guard["m"]
        for tc in range(TC):
            lm, ma = mk(tag + "L"), mk(tag + "A")
            ap_marks.append(ma)

            @op("y")
            def _(y, tc=tc, lm=lm, lg=lg):
                if src_marks is not None:
                    c.wait(y, src_marks[tc])
                if tc >= 2:
                    c.wait(y, ap_marks[tc - 2])
                else:
                    c.wait(y, statsmark)
                    c.wait(y, lg)
                c.dma(y, lambda: y.dma_start(
                    out=xcb2[:, 1024 * (tc % 2):1024 * (tc % 2) + 1024],
                    in_=xrows(src, tc)), "m", mark=lm)

            @op("v")
            def _(v, tc=tc, lm=lm, ma=ma, prev_mm=prev_mm,
                  g0=(last["scb1g"] if tc == 0 else None)):
                c.wait(v, lm)
                if tc == 0:
                    c.wait(v, statsmark)
                    c.wait(v, g0)
                c.wait(v, prev_mm)
                xv = xcb2[:, 1024 * (tc % 2):1024 * (tc % 2) + 1024]
                if inner:
                    v.drain()
                    v.tensor_scalar(scf1[:], xv, mvbuf[:, 2 * tc:2 * tc + 1],
                                    invbuf[:, tc:tc + 1], ALU.subtract,
                                    ALU.mult)
                    v.drain()
                    v.bn_stats(st6[:, 0:6], scf1[:, 0:512])
                    v.bn_stats(st6[:, 6:12], scf1[:, 512:1024])
                    v.drain()
                    v.bn_aggr(mv2[:, 2 * tc:2 * tc + 2], st6[:, 0:12])
                    c.inc(v.tensor_copy(scb1[:], scf1[:]), "V", mark=ma)
                else:
                    c.inc(v.tensor_scalar(scb1[:], xv,
                                          mvbuf[:, 2 * tc:2 * tc + 1],
                                          invbuf[:, tc:tc + 1], ALU.subtract,
                                          ALU.mult), "V", mark=ma)
            m_mm, m_ev = transposes(tag, tc, ma, scb1, dst_fm,
                                    fm_guard if tc == 0 else None)
            prev_mm = m_mm
            last["scb1g"] = m_mm
            ev_list.append(m_ev)
            last_ev = m_ev
        ln_guard["m"] = ap_marks[TC - 1]
        if not inner:
            return ev_list
        m_inv2 = batch_inv(tag + "2", mv2, inv2, ap_marks[TC - 1])
        return ev_list, m_inv2

    def xn2_pass(tag, src, m_inv2, dst_fm, fm_guard, lnw_ready, scb2):
        ap_marks = []
        last_ev = None
        prev_mm = None
        for tc in range(TC):
            lm, ma = mk(tag + "L"), mk(tag + "A")
            ap_marks.append(ma)

            @op("y")
            def _(y, tc=tc, lm=lm):
                if tc >= 2:
                    c.wait(y, ap_marks[tc - 2])
                else:
                    c.wait(y, m_inv2)
                c.dma(y, lambda: y.dma_start(
                    out=xcb[:, 1024 * (tc % 2):1024 * (tc % 2) + 1024],
                    in_=xrows(src, tc)), "m", mark=lm)

            @op("v")
            def _(v, tc=tc, lm=lm, ma=ma, prev_mm=prev_mm):
                c.wait(v, lm)
                if tc == 0:
                    c.wait(v, m_inv2)
                    c.wait(v, lnw_ready)
                c.wait(v, prev_mm)
                xv = xcb2[:, 1024 * (tc % 2):1024 * (tc % 2) + 1024]
                v.tensor_scalar(scf1[:], xv, mv1[:, 2 * tc:2 * tc + 1],
                                inv1[:, tc:tc + 1], ALU.subtract, ALU.mult)
                v.drain()
                v.tensor_scalar(scf1[:], scf1[:], mv2[:, 2 * tc:2 * tc + 1],
                                inv2[:, tc:tc + 1], ALU.subtract, ALU.mult)
                v.drain()
                v.tensor_tensor(scf1[:], scf1[:], lnwb[:], ALU.mult)
                v.drain()
                c.inc(v.tensor_tensor(scb2[:], scf1[:], lnbb[:], ALU.add),
                      "V", mark=ma)
            m_mm, m_ev = transposes(tag, tc, ma, scb2, dst_fm,
                                    fm_guard if tc == 0 else None)
            prev_mm = m_mm
            last_ev = m_ev
        return last_ev

    def pick(idx):
        return (pA, "pA") if idx % 2 == 0 else (pB, "pB")

    def tok_proj(tag, lhs_fm, lhs_ready, Wbuf, w_ready, out_sb,
                 bias_slot=None, bias_mark=None):
        ev_marks = []
        for tc in range(TC):
            ps, pn = pick(tc)
            m_mm, m_ev = mk(tag + "M"), mk(tag + "E")
            pfree = last[pn]
            last[pn] = m_ev
            ev_marks.append(m_ev)

            @op("t")
            def _(t, tc=tc, ps=ps, m_mm=m_mm, pfree=pfree):
                if isinstance(lhs_ready, list):
                    c.wait(t, lhs_ready[tc])
                elif tc == 0:
                    c.wait(t, lhs_ready)
                if tc == 0:
                    c.wait(t, w_ready)
                    c.wait(t, "set:v")
                    c.wait(t, bias_mark)
                c.wait(t, pfree)
                for dc in range(DC):
                    for nn in range(2):
                        stop = (bias_slot is None and dc == DC - 1)
                        mmi = t.matmul(
                            ps[:, 512 * nn:512 * (nn + 1)],
                            lhs_fm[:, T * dc + 128 * tc:T * dc + 128 * (tc + 1)],
                            Wbuf[:, 1024 * dc + 512 * nn:
                                 1024 * dc + 512 * (nn + 1)],
                            start=(dc == 0), stop=stop)
                if bias_slot is not None:
                    for nn in range(2):
                        mmi = t.matmul(
                            ps[:, 512 * nn:512 * (nn + 1)],
                            ones_row[0:1, 0:128],
                            brow[0:1, bias_slot * D + 512 * nn:
                                 bias_slot * D + 512 * (nn + 1)],
                            start=False, stop=True)
                c.inc(mmi, "T", mark=m_mm)

            @op("v")
            def _(v, tc=tc, ps=ps, m_mm=m_mm, m_ev=m_ev):
                c.wait(v, m_mm)
                c.inc(v.tensor_copy(out_sb[:, 1024 * tc:1024 * (tc + 1)],
                                    ps[:]), "V", mark=m_ev)
        return ev_marks

    def resid_out(tag, nk, emit_mms, resid_src, resid_dst, store_sem,
                  first_wait=(), tapname=None, load_guard=None, stm=None,
                  hook=None, slots=2, evm=None, xstage=None,
                  hook_apm=None, prev_stage=None, stage_guard=None):
        ev = []
        if stm is None:
            stm = [mk(tag + "W") for _ in range(TC)]
        rot = [(pA, "pA"), (pB, "pB"), (pP1, "pP1")][:slots]
        nst = len(xstage) if xstage else 2

        def xv_of(tc):
            if xstage:
                return xstage[tc % nst]
            return xcb[:, 1024 * (tc % 2):1024 * (tc % 2) + 1024]

        for tc in range(TC):
            ps, pn = rot[tc % slots]
            m_mm, m_ld, m_ev = (mk(tag + "M"), mk(tag + "R"),
                                mk(tag + "E"))
            if evm is not None:
                m_ev = evm[tc]
            m_st = stm[tc]
            pfree = last[pn]
            last[pn] = m_ev
            ev.append(m_ev)

            @op("t")
            def _(t, tc=tc, ps=ps, m_mm=m_mm, pfree=pfree):
                if tc == 0:
                    for w in first_wait:
                        c.wait(t, w)
                c.wait(t, pfree)
                emit_mms(t, tc, ps)
                c.inc(t.last_instr if False else _mmi[0], "T", mark=m_mm)

            @op("y")
            def _(y, tc=tc, m_ld=m_ld):
                if tc >= nst:
                    c.wait(y, stm[tc - nst])
                    if tapname is not None:
                        c.wait(y, stm[tc - nst] + "t")
                    if hook_apm is not None:
                        c.wait(y, hook_apm[tc - nst])
                else:
                    if load_guard is not None:
                        for m in (load_guard
                                  if isinstance(load_guard, (list, tuple))
                                  else (load_guard,)):
                            c.wait(y, m)
                    if prev_stage is not None:
                        for lst in prev_stage:
                            c.wait(y, lst[TC - nst + tc])
                    if tc >= 2 and stage_guard is not None:
                        c.wait(y, stage_guard)
                c.dma(y, lambda: y.dma_start(
                    out=xv_of(tc),
                    in_=xrows(resid_src, tc)), "m", mark=m_ld)

            @op("v")
            def _(v, tc=tc, ps=ps, m_mm=m_mm, m_ld=m_ld, m_ev=m_ev):
                c.wait(v, m_mm)
                c.wait(v, m_ld)
                xv = xv_of(tc)
                c.inc(v.tensor_tensor(xv, ps[:], xv, ALU.add), "V", mark=m_ev)

            @op("s")
            def _(s, tc=tc, m_ev=m_ev, m_st=m_st):
                c.wait(s, m_ev)
                c.dma(s, lambda: s.dma_start(
                    out=xrows(resid_dst, tc),
                    in_=xv_of(tc)), "n", mark=m_st)
                if tapname is not None:
                    c.dma(s, lambda: s.dma_start(
                        out=xrows(dtap[tapname], tc),
                        in_=xv_of(tc)), "n", mark=m_st + "t")
            if hook is not None:
                hook()
        return ev, stm

    _mmi = [None]

    def fm_proj(tag, Wbuf, w_ready, rhs_fm, rhs_ready, out_fm,
                bias_slot=None, bias_mark=None):
        evs = []
        for ec in range(DC):
            for th in range(2):
                idx = ec * 2 + th
                ps, pn = pick(idx)
                m_mm, m_ev = mk(tag + "M"), mk(tag + "E")
                pfree = last[pn]
                last[pn] = m_ev

                @op("t")
                def _(t, ec=ec, th=th, idx=idx, ps=ps, m_mm=m_mm, pfree=pfree):
                    if isinstance(rhs_ready, list):
                        if idx <= 1:
                            c.wait(t, rhs_ready[8 * th + 7])
                    elif idx == 0:
                        c.wait(t, rhs_ready)
                    if idx == 0:
                        c.wait(t, w_ready)
                        c.wait(t, "set:v")
                        c.wait(t, bias_mark)
                    c.wait(t, pfree)
                    for dc in range(DC):
                        for nn in range(2):
                            stop = (bias_slot is None and dc == DC - 1)
                            mmi = t.matmul(
                                ps[:, 512 * nn:512 * (nn + 1)],
                                Wbuf[:, 1024 * dc + 128 * ec:
                                     1024 * dc + 128 * (ec + 1)],
                                rhs_fm[:, T * dc + 1024 * th + 512 * nn:
                                       T * dc + 1024 * th + 512 * (nn + 1)],
                                start=(dc == 0), stop=stop)
                    if bias_slot is not None:
                        bap = AP(brow.ap().tensor,
                                 brow.ap().offset + bias_slot * D + 128 * ec,
                                 [[brow.ap().ap[0][0], 1], [1, 128]])
                        for nn in range(2):
                            mmi = t.matmul(ps[:, 512 * nn:512 * (nn + 1)],
                                           bap, ones_row[0:1, 0:512],
                                           start=False, stop=True)
                    c.inc(mmi, "T", mark=m_mm)

                @op("v")
                def _(v, ec=ec, th=th, ps=ps, m_mm=m_mm, m_ev=m_ev):
                    c.wait(v, m_mm)
                    c.inc(v.tensor_copy(
                        out_fm[:, T * ec + 1024 * th:T * ec + 1024 * (th + 1)],
                        ps[:]), "V", mark=m_ev)
                evs.append(m_ev)
        return evs

    Wb1 = salloc("Wb1", [128, DC * 1024], BF16, side="right")
    act1 = salloc("act1", [128, TC * 1024], BF16, side="right")
    act3 = salloc("act3", [128, TC * 1024], BF16, side="right")
    wload(Wb0, "aWk", "w:aWk")
    wload(Wb1, "aWv", "w:aWv")
    bload(0, "abk", "b:abk")

    mA_fm = ln_fused("a2", dram["x"], mv1, inv1, fmA, None)
    kev = tok_proj("a3", fmA, mA_fm, Wb0, "w:aWk", act1,
                   bias_slot=0, bias_mark="b:abk")
    bload(0, "abv", "b:abv", waits=(kev[TC - 1],))
    vev = tok_proj("a4", fmA, mA_fm, Wb1, "w:aWv", act3,
                   bias_slot=0, bias_mark="b:abv")
    wload(Wb0, "aWq", "w:aWq", waits=(kev[TC - 1],))
    bload(0, "abq", "b:abq", waits=(vev[TC - 1],))

    srow = salloc("srow", [1, D], BF16, side="right")
    gch = salloc("gch", [128, 2 * 1024], BF16, side="right")
    rdb = salloc("rdb", [128, 4 * 1024], BF16, side="right")

    g_ev, g_wr = [], []
    for ic in range(DC):
        ps, pn = pick(ic)
        m_mm, m_ev, m_wr = mk("a5M"), mk("a5E"), mk("a5W")
        pfree = last[pn]
        last[pn] = m_ev

        @op("t")
        def _(t, ic=ic, ps=ps, m_mm=m_mm, pfree=pfree, act1=act1,
              act3=act3):
            if ic == 0:
                c.wait(t, kev[TC - 1])
                c.wait(t, vev[TC - 1])
            c.wait(t, pfree)
            for tc in range(TC):
                for nn in range(2):
                    mmi = t.matmul(
                        ps[:, 512 * nn:512 * (nn + 1)],
                        act1[:, 1024 * tc + 128 * ic:1024 * tc + 128 * (ic + 1)],
                        act3[:, 1024 * tc + 512 * nn:1024 * tc + 512 * (nn + 1)],
                        start=(tc == 0), stop=(tc == TC - 1))
            c.inc(mmi, "T", mark=m_mm)

        @op("v")
        def _(v, ic=ic, ps=ps, m_mm=m_mm, m_ev=m_ev):
            c.wait(v, m_mm)
            if ic >= 2:
                c.wait(v, g_wr[ic - 2])
                c.wait(v, g_wr[ic - 2] + "a")
            c.inc(v.tensor_copy(gch[:, 1024 * (ic % 2):1024 * (ic % 2) + 1024],
                                ps[:]), "V", mark=m_ev)

        @op("y")
        def _(y, ic=ic, m_ev=m_ev, m_wr=m_wr):
            c.wait(y, m_ev)
            base = ic * (128 * 2177)
            gv = gch[:, 1024 * (ic % 2):1024 * (ic % 2) + 1024]
            c.dma(y, lambda: y.dma_start(out=AP(dskew, base, [[2177, 128], [1, 1024]]),
                              in_=gv), "m", mark=m_wr + "a")
            c.dma(y, lambda: y.dma_start(out=AP(dskew, base + 1024,
                                     [[2177, 128], [1, 1024]]), in_=gv), "m", mark=m_wr)
        g_ev.append(m_ev)
        g_wr.append(m_wr)

    sfree("rdb")
    sfree("gch")
    sfree("srow")
    sfree("act3")
    sfree("act1")
    Qcirc = salloc("Qcirc", [128, 2 * D], BF16, side="right")
    act2 = salloc("act2", [128, TC * 1024], BF16)

    qev = fm_proj("a7", Wb0, "w:aWq", fmA, mA_fm, act2,
                  bias_slot=0, bias_mark="b:abq")

    sa_marks = []
    for ic in range(DC):
        m_rd, m_ac = mk("a6R"), mk("a6A")
        sa_marks.append(m_ac)

        @op("y")
        def _(y, ic=ic, m_rd=m_rd):
            c.wait(y, g_wr[ic])
            c.wait(y, g_wr[ic] + "a")
            if ic >= 4:
                c.wait(y, sa_marks[ic - 4])
            base = ic * (128 * 2177) + 1024
            off = 128 * ic
            sl = rdb[:, 1024 * (ic % 4):1024 * (ic % 4) + 1024]
            if off == 0:
                c.dma(y, lambda: y.dma_start(out=sl,
                                  in_=AP(dskew, base, [[2176, 128], [1, 1024]])), "m", mark=m_rd)
            else:
                c.dma(y, lambda: y.dma_start(out=sl[:, off:1024],
                                   in_=AP(dskew, base,
                                          [[2176, 128], [1, 1024 - off]])), "m", mark=m_rd + "a")
                c.dma(y, lambda: y.dma_start(out=sl[:, 0:off],
                                  in_=AP(dskew, base + 1024 - off,
                                         [[2176, 128], [1, off]])), "m", mark=m_rd)

        @op("v")
        def _(v, ic=ic, m_rd=m_rd, m_ac=m_ac):
            c.wait(v, m_rd)
            if ic > 0:
                c.wait(v, m_rd + "a")
            sl = rdb[:, 1024 * (ic % 4):1024 * (ic % 4) + 1024]
            if ic == 0:
                c.wait(v, "setup:g")
                c.inc(v.tensor_copy(scf1[:], sl), "V", mark=m_ac)
            else:
                v.drain()
                c.inc(v.tensor_tensor(scf1[:], scf1[:], sl, ALU.add), "V",
                      mark=m_ac)

    @op("v")
    def _(v):
        c.wait(v, sa_marks[DC - 1])
        v.drain()
        c.inc(v.tensor_copy(scb1[:], scf1[:]), "V", mark="a6C")

    m_pP1free = last["pP1"]
    last["pP1"] = "a6srow"

    @op("t")
    def _(t):
        c.wait(t, "a6C")
        c.wait(t, m_pP1free)
        t.matmul(pP1[0:1, 0:512], ones_col[:], scb1[:, 0:512],
                 start=True, stop=True)
        c.inc(t.matmul(pP1[0:1, 512:1024], ones_col[:], scb1[:, 512:1024],
                       start=True, stop=True), "T", mark="a6S")

    @op("s")
    def _(s):
        c.wait(s, "a6S")
        c.inc(s.activation(srow[0:1, 0:D], pP1[0:1, 0:D], AF.Copy), "S",
              mark="a6srow")

    @op("y")
    def _(y):
        c.wait(y, "a6srow")
        for r in range(3):
            c.dma(y, lambda r=r: y.dma_start(out=AP(ds3, r * D, [[D, 1], [1, D]]),
                              in_=srow[0:1, 0:D]), "m", mark=f"a6s3:{r}")
        for r in range(3):
            c.wait(y, f"a6s3:{r}")
        c.dma(y, lambda: y.dma_start(out=Qcirc[:], in_=AP(ds3, 0, [[1, 128], [1, 2048]])), "m", mark="a6Q")

    def a8_mms(t, tc, ps):
        for a in range(DC):
            for nn in range(2):
                _mmi[0] = t.matmul(
                    ps[:, 512 * nn:512 * (nn + 1)],
                    act2[:, T * a + 128 * tc:T * a + 128 * (tc + 1)],
                    Qcirc[:, 128 * a + 512 * nn:128 * a + 512 * (nn + 1)],
                    start=(a == 0), stop=(a == DC - 1))

    a8_stm = [mk("a8W") for _ in range(TC)]
    a8_evm = [mk("a8E") for _ in range(TC)]
    xs4 = [xcb[:, 0:1024], xcb[:, 1024:2048],
           xcb2[:, 0:1024], xcb2[:, 1024:2048]]
    xg_a = ln_guard["m"]
    lnB = ln_fused("ba", dxb0, mv1, inv1, fmA, qev[15], src_marks=a8_evm,
                   lazy=True, stage=xs4)
    a8_ev, a8_st = resid_out("a8", DC, a8_mms, dram["x"], dxb0, "dB",
                             first_wait=(qev[15], "a6Q"),
                             tapname="t_xa" if dbg else None,
                             load_guard=qev[15], stm=a8_stm, evm=a8_evm,
                             hook=lnB["emit"], slots=3, xstage=xs4,
                             hook_apm=lnB["ap"], stage_guard=xg_a)
    lnB["flush"]()

    sfree("Qcirc")

    def ffn_phase(tag, src, dst, tapname, store_sem, prev_store, fm_guard,
                  wguard, prev_tap=False, lnobj=None, out_stm=None,
                  out_hook=None, out_ln_guard=None, out_kw=None):
        wgb = salloc(tag + "wgb", [128, DC * NE], BF16, side="right")
        A_all = salloc(tag + "Aall", [128, DC * 128], BF16, side="right")
        B_all = salloc(tag + "Ball", [128, D], BF16, side="right")
        cwT = salloc(tag + "cwT", [8, T], BF16, side="right")
        cwx = salloc(tag + "cwx", [128, T], BF16, side="right")

        @op("g")
        def _(g):
            for m in (prev_store[-2:] if prev_store else ()):
                c.wait(g, m)
            c.dma(g, lambda: g.dma_start(
                out=wgb.ap().rearrange("p (ch e) -> p ch e", ch=DC),
                in_=AP(dram["fWg"], 0, [[NE, 128], [128 * NE, DC], [1, NE]])), "g", mark=tag + "w:fWg")
            for ch in range(DC):
                c.dma(g, lambda: g.dma_start(
                    out=AP(A_all.ap().tensor, 128 * ch,
                           [[A_all.ap().ap[0][0], 128], [NR, NE], [1, NR]]),
                    in_=AP(dram["fA"], 128 * NR * ch,
                           [[NR, 128], [D * NR, NE], [1, NR]])), "g", mark=tag + f"w:fA{ch}")
            c.dma(g, lambda: g.dma_start(out=B_all[:],
                              in_=AP(dram["fB"], 0, [[D, 128], [1, D]])), "g", mark=tag + "w:fB")

        if lnobj is None:
            m_fm = ln_fused(tag + "a", src, mv1, inv1, fmA, fm_guard,
                            src_marks=prev_store)
        else:
            m_fm = lnobj["ev"]
        lg_ev = []
        for tc in range(TC):
            ps, pn = pick(tc)
            m_mm, m_ev = mk(tag + "gM"), mk(tag + "gE")
            pfree = last[pn]
            last[pn] = m_ev

            @op("t")
            def _(t, tc=tc, ps=ps, m_mm=m_mm, pfree=pfree):
                c.wait(t, m_fm[tc])
                if tc == 0:
                    c.wait(t, tag + "w:fWg")
                c.wait(t, pfree)
                for dc in range(DC):
                    mmi = t.matmul(
                        ps[:, 0:NE],
                        fmA[:, T * dc + 128 * tc:T * dc + 128 * (tc + 1)],
                        wgb[:, NE * dc:NE * (dc + 1)],
                        start=(dc == 0), stop=(dc == DC - 1))
                c.inc(mmi, "T", mark=m_mm)

            @op("v")
            def _(v, tc=tc, ps=ps, m_mm=m_mm, m_ev=m_ev):
                c.wait(v, m_mm)
                if tc == 0:
                    c.wait(v, "setup:g")
                    for m in (prev_store[-2:] if prev_store else ()):
                        c.wait(v, m)
                        if prev_tap:
                            c.wait(v, m + "t")
                c.inc(v.tensor_copy(gl[:, NE * tc:NE * (tc + 1)],
                                    ps[:, 0:NE]), "V", mark=m_ev)
            lg_ev.append(m_ev)

        m_sh, m_ex, m_ind, m_em2, m_cwf = (mk(tag + "sh"), mk(tag + "ex"),
                                           mk(tag + "in"), mk(tag + "em"),
                                           mk(tag + "cw"))

        @op("v")
        def _(v):
            c.wait(v, lg_ev[TC - 1])
            v.drain()
            glv = gl.rearrange("p (t e) -> p t e", e=NE)
            v.reduce_max(gm[:, 0:TC], glv, axis=AX.X)
            v.drain()
            m1b = AP(gm.tensor, gm.offset,
                     [[gm.ap[0][0], 128], [1, TC], [0, NE]])
            c.inc(v.tensor_tensor(
                gsh.rearrange("p (t e) -> p t e", e=NE), glv, m1b,
                ALU.subtract), "V", mark=m_sh)

        @op("s")
        def _(s):
            c.wait(s, m_sh)
            c.inc(s.activation(gl[:, 0:NE * TC], gsh[:, 0:NE * TC], AF.Exp),
                  "S", mark=m_ex)

        @op("v")
        def _(v):
            c.wait(v, m_ex)
            v.tensor_scalar(gih[:, 0:NE * TC], gsh[:, 0:NE * TC], 0.0, None,
                            ALU.is_ge)
            v.drain()
            v.scalar_tensor_tensor(gih[:, 0:NE * TC], gih[:, 0:NE * TC],
                                   -1e30, gsh[:, 0:NE * TC], ALU.mult,
                                   ALU.add)
            v.drain()
            v.reduce_max(gm[:, TC:2 * TC],
                         gih.rearrange("p (t e) -> p t e", e=NE),
                         axis=AX.X)
            v.drain()
            m2b = AP(gm.tensor, gm.offset + TC,
                     [[gm.ap[0][0], 128], [1, TC], [0, NE]])
            c.inc(v.tensor_tensor(
                gih.rearrange("p (t e) -> p t e", e=NE),
                gsh.rearrange("p (t e) -> p t e", e=NE), m2b,
                ALU.is_ge), "V", mark=m_ind)

        @op("s")
        def _(s):
            c.wait(s, m_ind)
            c.inc(s.activation(gm[:, 2 * TC:3 * TC], gm[:, TC:2 * TC],
                               AF.Exp), "S", mark=m_em2)

        @op("v")
        def _(v):
            c.wait(v, m_em2)
            v.tensor_scalar(gm[:, 3 * TC:4 * TC], gm[:, 2 * TC:3 * TC], 1.0,
                            None, ALU.add)
            v.drain()
            v.reciprocal(gm[:, 4 * TC:5 * TC], gm[:, 3 * TC:4 * TC])
            v.drain()
            dinvb = AP(gm.tensor, gm.offset + 4 * TC,
                       [[gm.ap[0][0], 128], [1, TC], [0, NE]])
            v.tensor_tensor(gl.rearrange("p (t e) -> p t e", e=NE),
                            gl.rearrange("p (t e) -> p t e", e=NE),
                            dinvb, ALU.mult)
            v.drain()
            c.inc(v.tensor_tensor(cwb[:, 0:NE * TC], gl[:, 0:NE * TC],
                                  gih[:, 0:NE * TC], ALU.mult), "V",
                  mark=m_cwf)

        m_hm = []
        hp_guard = [last["pP1"], last["pB"]]
        for th in range(2):
            m_mm = mk(tag + "hM")
            m_hm.append(m_mm)
            hps = pP1 if th == 0 else pB

            @op("t")
            def _(t, th=th, hps=hps, m_mm=m_mm, pfree=hp_guard[th]):
                if th == 0:
                    for ch in range(DC):
                        c.wait(t, tag + f"w:fA{ch}")
                c.wait(t, m_fm[8 * th + 7])
                c.wait(t, pfree)
                for dc in range(DC):
                    for nn in range(2):
                        mmi = t.matmul(
                            hps[:, 512 * nn:512 * (nn + 1)],
                            A_all[:, 128 * dc:128 * (dc + 1)],
                            fmA[:, T * dc + 1024 * th + 512 * nn:
                                T * dc + 1024 * th + 512 * (nn + 1)],
                            start=(dc == 0), stop=(dc == DC - 1))
                c.inc(mmi, "T", mark=m_mm)

        ct_last = None
        for tc in range(TC):
            m_tm, m_te = mk(tag + "cM"), mk(tag + "cE")

            half = tp_ping["i"] % 2
            tp_ping["i"] += 1
            prev = tp_ping["prev"][half]
            tp_ping["prev"][half] = m_te
            base = 1024 * half

            @op("t")
            def _(t, tc=tc, m_tm=m_tm, prev=prev, base=base):
                if tc == 0:
                    c.wait(t, m_cwf)
                    c.wait(t, "id:done")
                c.wait(t, prev)
                c.inc(t.transpose(pTb[0:NE, base:base + 128],
                                  cwb[:, NE * tc:NE * (tc + 1)], ident[:]),
                      "T", mark=m_tm)

            @op("v")
            def _(v, tc=tc, m_tm=m_tm, m_te=m_te, base=base):
                c.wait(v, m_tm)
                c.inc(v.tensor_copy(cwT[0:NE, 128 * tc:128 * (tc + 1)],
                                    pTb[0:NE, base:base + 128]), "V",
                      mark=m_te)
            ct_last = m_te

        cx_ev = []
        for half in range(2):
            m_mm, m_ev = mk(tag + "xM"), mk(tag + "xE")
            pfree = last["pA"]
            last["pA"] = m_ev

            @op("t")
            def _(t, half=half, m_mm=m_mm, pfree=pfree):
                if half == 0:
                    c.wait(t, ct_last)
                    c.wait(t, "e8:done")
                c.wait(t, pfree)
                for nn in range(2):
                    mmi = t.matmul(pA[:, 512 * nn:512 * (nn + 1)], E8[:],
                                   cwT[0:NE, 1024 * half + 512 * nn:
                                       1024 * half + 512 * (nn + 1)],
                                   start=True, stop=True)
                c.inc(mmi, "T", mark=m_mm)

            @op("v")
            def _(v, half=half, m_mm=m_mm, m_ev=m_ev):
                c.wait(v, m_mm)
                c.inc(v.tensor_copy(cwx[:, 1024 * half:1024 * (half + 1)],
                                    pA[:]), "V", mark=m_ev)
            cx_ev.append(m_ev)

        hw_ev = []
        for th in range(2):
            m_ev = mk(tag + "hE")
            hps = pP1 if th == 0 else pB
            if th == 0:
                last["pP1"] = m_ev
            else:
                last["pB"] = m_ev

            @op("v")
            def _(v, th=th, hps=hps, m_ev=m_ev):
                c.wait(v, m_hm[th])
                c.wait(v, cx_ev[th])
                c.inc(v.tensor_tensor(act2[:, 1024 * th:1024 * (th + 1)],
                                      hps[:], cwx[:, 1024 * th:1024 * (th + 1)],
                                      ALU.mult), "V", mark=m_ev)
            hw_ev.append(m_ev)

        def o_mms(t, tc, ps):
            for nn in range(2):
                _mmi[0] = t.matmul(ps[:, 512 * nn:512 * (nn + 1)],
                                   act2[:, 128 * tc:128 * (tc + 1)],
                                   B_all[:, 512 * nn:512 * (nn + 1)],
                                   start=True, stop=True)

        oguard = [hw_ev[1]] + (prev_store[-2:] if prev_store else [])
        if prev_tap:
            oguard += [m + "t" for m in prev_store[-2:]]
        if out_ln_guard is not None:
            out_ln_guard["m"] = hw_ev[1]
        o_ev, o_st = resid_out(tag + "o", 1, o_mms, src, dst, store_sem,
                               first_wait=(hw_ev[1], tag + "w:fB"),
                               tapname=tapname, load_guard=tuple(oguard),
                               stm=out_stm, hook=out_hook,
                               slots=(3 if out_hook is not None else 2),
                               **(out_kw or {}))
        sfree(tag + "cwx")
        sfree(tag + "cwT")
        sfree(tag + "Ball")
        sfree(tag + "Aall")
        sfree(tag + "wgb")
        return list(o_st), hw_ev[1]

    b_stm = [mk("boW") for _ in range(TC)]
    b_evm = [mk("boE") for _ in range(TC)]
    lnC_guard = {"m": None}
    xg_b = ln_guard["m"]
    lnC = ln_fused("c2", dxb1, mv1, inv1, fmA, lnC_guard, src_marks=b_evm,
                   inner=True, lazy=True, stage=xs4)
    mB_st, mB_fmlast = ffn_phase("b", dxb0, dxb1, "t_xb" if dbg else None,
                                 "dB", list(a8_st), qev[15], "w:aWv",
                                 prev_tap=dbg, lnobj=lnB, out_stm=b_stm,
                                 out_hook=lnC["emit"],
                                 out_ln_guard=lnC_guard,
                                 out_kw=dict(evm=b_evm, xstage=xs4,
                                             hook_apm=lnC["ap"],
                                             prev_stage=(a8_stm, lnB["ap"]),
                                             stage_guard=xg_b))
    lnC["flush"]()

    act3 = salloc("act3", [128, TC * 1024], BF16, side="right")
    fmBq = salloc("fmBq", [128, DC * 512], BF16, side="right")
    lnwb = salloc("lnwb", [128, D], F32, side="right")
    lnbb = salloc("lnbb", [128, D], F32, side="right")
    wload(Wb0, "bWq", "w:bWq", waits=(qev[15],))

    @op("g")
    def _(g):
        c.dma(g, lambda: g.dma_start(out=lnwb[:],
                          in_=AP(dram["blnw"], 0, [[0, 128], [1, D]])), "g", mark="c:lnwb")
        c.dma(g, lambda: g.dma_start(out=lnbb[:],
                          in_=AP(dram["blnb"], 0, [[0, 128], [1, D]])), "g", mark="c:lnbb")

    mC_fm, mC_i2 = lnC["ev"], lnC["i2"]

    def xn2_quarter(tag, qi, fm_guard, prev_ap):
        ap_marks = []
        prev_mm = None
        last_ev = None
        for tcl in range(TC // 4):
            tc = (TC // 4) * qi + tcl
            lm, ma = mk(tag + "L"), mk(tag + "A")
            ap_marks.append(ma)

            @op("y")
            def _(y, tc=tc, tcl=tcl, lm=lm):
                c.wait(y, mB_st[tc])
                if dbg:
                    c.wait(y, mB_st[tc] + "t")
                if tcl >= 2:
                    c.wait(y, ap_marks[tcl - 2])
                else:
                    c.wait(y, prev_ap)
                    c.wait(y, mB_st[14 + (tc % 2)])
                    if dbg:
                        c.wait(y, mB_st[14 + (tc % 2)] + "t")
                c.dma(y, lambda: y.dma_start(
                    out=xcb2[:, 1024 * (tc % 2):1024 * (tc % 2) + 1024],
                    in_=xrows(dxb1, tc)), "m", mark=lm)

            @op("v")
            def _(v, tc=tc, tcl=tcl, lm=lm, ma=ma, prev_mm=prev_mm,
                  g0=(last["scb1g"] if tcl == 0 else None)):
                c.wait(v, lm)
                c.wait(v, mC_i2[tc])
                if tcl == 0:
                    c.wait(v, "c:lnwb")
                    c.wait(v, "c:lnbb")
                    c.wait(v, g0)
                c.wait(v, prev_mm)
                xv = xcb2[:, 1024 * (tc % 2):1024 * (tc % 2) + 1024]
                v.tensor_scalar(scf1[:], xv, mv1[:, 2 * tc:2 * tc + 1],
                                inv1[:, tc:tc + 1], ALU.subtract, ALU.mult)
                v.drain()
                v.tensor_scalar(scf1[:], scf1[:], mv2[:, 2 * tc:2 * tc + 1],
                                inv2[:, tc:tc + 1], ALU.subtract, ALU.mult)
                v.drain()
                v.tensor_tensor(scf1[:], scf1[:], lnwb[:], ALU.mult)
                v.drain()
                c.inc(v.tensor_tensor(scb1[:], scf1[:], lnbb[:], ALU.add),
                      "V", mark=ma)
            m_mm, m_ev = transposes(tag, tc, ma, scb1, fmBq,
                                    fm_guard if tcl == 0 else None,
                                    tstride=512, toff=128 * tcl)
            prev_mm = m_mm
            last["scb1g"] = m_mm
            last_ev = m_ev
        return last_ev, ap_marks[-1]

    def fm_proj_quarter(tag, Wbuf, w_ready, rhs_ready, out_fm, toff,
                        extra_first=()):
        evs, mms = [], []
        for ec in range(DC):
            ps, pn = pick(ec)
            m_mm, m_ev = mk(tag + "M"), mk(tag + "E")
            pfree = last[pn]
            last[pn] = m_ev

            @op("t")
            def _(t, ec=ec, ps=ps, m_mm=m_mm, pfree=pfree):
                if ec == 0:
                    c.wait(t, w_ready)
                    c.wait(t, rhs_ready)
                    for w in extra_first:
                        c.wait(t, w)
                c.wait(t, pfree)
                for dc in range(DC):
                    mmi = t.matmul(
                        ps[:, 0:512],
                        Wbuf[:, 1024 * dc + 128 * ec:1024 * dc + 128 * (ec + 1)],
                        fmBq[:, 512 * dc:512 * (dc + 1)],
                        start=(dc == 0), stop=(dc == DC - 1))
                c.inc(mmi, "T", mark=m_mm)

            @op("v")
            def _(v, ec=ec, ps=ps, m_mm=m_mm, m_ev=m_ev):
                c.wait(v, m_mm)
                c.inc(v.tensor_copy(out_fm[:, T * ec + toff:T * ec + toff + 512],
                                    ps[:, 0:512]), "V", mark=m_ev)
            evs.append(m_ev)
            mms.append(m_mm)
        return evs, mms

    qbev, kbev = [], []
    fmBq_guard = None
    prev_ap = mC_fm[-1]
    for qi in range(4):
        m_x2, prev_ap = xn2_quarter(f"c4q{qi}", qi, fmBq_guard, prev_ap)
        evs, mms = fm_proj_quarter(f"c5q{qi}", Wb0, "w:bWq", m_x2, act2,
                                   512 * qi)
        qbev += evs
        fmBq_guard = mms[-1]
    q_guard = fmBq_guard
    wload(Wb1, "bWk", "w:bWk", waits=(vev[TC - 1],))
    for qi in range(4):
        m_x2, prev_ap = xn2_quarter(f"c4k{qi}", qi, fmBq_guard, prev_ap)
        evs, mms = fm_proj_quarter(f"c6q{qi}", Wb1, "w:bWk", m_x2, act3,
                                   512 * qi)
        kbev += evs
        fmBq_guard = mms[-1]
    wload(Wb0, "bWv", "w:bWv", waits=(q_guard,))

    sfree("lnbb")
    sfree("lnwb")
    sfree("fmBq")
    act1 = salloc("act1", [128, TC * 1024], BF16, side="right")
    vbev = [mk("c3E") for _ in range(TC)]

    def emit_c3(tc):
        ps, pn = pick(tc)
        m_mm, m_ev = mk("c3M"), vbev[tc]
        pfree = last[pn]
        last[pn] = m_ev

        @op("t")
        def _(t, tc=tc, ps=ps, m_mm=m_mm, pfree=pfree):
            c.wait(t, mC_fm[tc])
            if tc == 0:
                c.wait(t, "w:bWv")
                c.wait(t, "set:v")
            c.wait(t, pfree)
            for dc in range(DC):
                for nn in range(2):
                    mmi = t.matmul(
                        ps[:, 512 * nn:512 * (nn + 1)],
                        fmA[:, T * dc + 128 * tc:T * dc + 128 * (tc + 1)],
                        Wb0[:, 1024 * dc + 512 * nn:1024 * dc + 512 * (nn + 1)],
                        start=(dc == 0), stop=(dc == DC - 1))
            c.inc(mmi, "T", mark=m_mm)

        @op("v")
        def _(v, tc=tc, ps=ps, m_mm=m_mm, m_ev=m_ev):
            c.wait(v, m_mm)
            c.inc(v.tensor_copy(act1[:, 1024 * tc:1024 * (tc + 1)],
                                ps[:]), "V", mark=m_ev)

    wload(Wb1, "bWo", "w:bWo", waits=(fmBq_guard,))
    bload(0, "bbo", "b:bbo", waits=(qev[15],))

    gatekT = salloc("gatekT", [128, 16 * TC], F32, side="right")
    gstage = salloc("gstage", [34, 1024], BF16, side="right")
    kping = xcb2[:, 0:1024]

    kg_ev = [None, None]
    k_sq = {}; k_ss = {}; k_rs = {}; k_sc = {}
    k_gm = {}; k_ge = {}; k_tp = {}; k_te = {}
    k_base = {}
    g0_scb = last["scb1g"]
    pP1_0 = last["pP1"]
    gps_0 = [last["pA"], last["pB"]]
    ptb_0 = [tp_ping["prev"][0], tp_ping["prev"][1]]

    def k_gate_mm(n):
        m_gm = mk("c7kG")
        k_gm[n] = m_gm

        @op("t")
        def _(t, n=n, m_gm=m_gm,
              gfree=(None if n else (last["pP2"], last["pTb"]))):
            c.wait(t, k_sc[n])
            if n == 0:
                c.wait(t, "setup:g")
                for m in gfree:
                    c.wait(t, m)
            else:
                c.wait(t, k_te.get(n - 1))
            base = k_base[n]
            for nn in range(2):
                mmi = t.matmul(pP2[0:2, 512 * nn:512 * (nn + 1)],
                               gk2[:, 0:2],
                               act3[:, base + 512 * nn:
                                    base + 512 * (nn + 1)],
                               start=True, stop=True)
            c.inc(mmi, "T", mark=m_gm)

        m_ge = mk("c7kH")
        k_ge[n] = m_ge

        @op("v")
        def _(v, n=n, m_ge=m_ge):
            c.wait(v, k_gm[n])
            c.wait(v, k_tp.get(n - 2))
            gs = 32 * (n % 2)
            c.inc(v.tensor_copy(gstage[gs:gs + 2, :], pP2[0:2, :]), "V",
                  mark=m_ge)

    def k_gate_tp(n):
        m_tp, m_te = mk("c7kT"), mk("c7kE")
        k_tp[n] = m_tp
        k_te[n] = m_te
        half = n % 2
        tb = 1024 * half
        tb_prev = k_te.get(n - 2, ptb_0[half])
        tp_ping["prev"][half] = m_te
        th_, ec = n // DC, n % DC

        @op("t")
        def _(t, n=n, tb=tb, m_tp=m_tp, tb_prev=tb_prev):
            c.wait(t, k_ge[n])
            if n == 0:
                c.wait(t, "id:done")
            c.wait(t, tb_prev)
            gs = 32 * (n % 2)
            for j in range(8):
                mmi = t.transpose(pTb[:, tb + 2 * j:tb + 2 * (j + 1)],
                                  gstage[gs:gs + 2, 128 * j:128 * (j + 1)],
                                  ident[gs:gs + 2, gs:gs + 2])
            c.inc(mmi, "T", mark=m_tp)

        @op("v")
        def _(v, th_=th_, ec=ec, m_tp=m_tp, m_te=m_te, tb=tb):
            c.wait(v, m_tp)
            gT = gatekT.ap()
            dap = AP(gT.tensor, gT.offset + 128 * th_ + 2 * ec,
                     [[gT.ap[0][0], 128], [16, 8], [1, 2]])
            c.inc(v.tensor_copy(
                dap,
                pTb[:, tb:tb + 16].rearrange("p (j q) -> p j q", j=8)),
                "V", mark=m_te)
        if ec == DC - 1:
            kg_ev[th_] = m_te

    for th_ in range(2):
        for ec in range(DC):
            n = th_ * DC + ec
            base = T * ec + 1024 * th_
            k_base[n] = base
            m_sq, m_ss, m_rs, m_sc = (mk("c7kQ"), mk("c7kS"), mk("c7kR"),
                                      mk("c7kC"))
            k_sq[n] = m_sq; k_ss[n] = m_ss; k_rs[n] = m_rs; k_sc[n] = m_sc
            scfb = scf1 if n % 2 == 0 else kping

            @op("s")
            def _(s, n=n, th_=th_, ec=ec, base=base, m_sq=m_sq):
                c.wait(s, kbev[8 * 2 * th_ + ec])
                c.wait(s, kbev[8 * (2 * th_ + 1) + ec])
                if n == 0:
                    c.wait(s, g0_scb)
                c.wait(s, k_ss.get(n - 1))
                c.inc(s.activation(scb1[:], act3[:, base:base + 1024],
                                   AF.Square), "S", mark=m_sq)

            @op("t")
            def _(t, n=n, m_sq=m_sq, m_ss=m_ss,
                  pfree=(pP1_0 if n == 0 else k_rs[n - 1])):
                c.wait(t, m_sq)
                if n == 0:
                    c.wait(t, "e2:done")
                c.wait(t, pfree)
                for nn in range(2):
                    mmi = t.matmul(pP1[:, 512 * nn:512 * (nn + 1)],
                                   E2rep[:],
                                   scb1[:, 512 * nn:512 * (nn + 1)],
                                   start=True, stop=True)
                c.inc(mmi, "T", mark=m_ss)

            @op("s")
            def _(s, n=n, scfb=scfb, m_ss=m_ss, m_rs=m_rs):
                c.wait(s, m_ss)
                c.wait(s, k_sc.get(n - 2))
                c.inc(s.activation(scfb[:], pP1[:], AF.Sqrt), "S",
                      mark=m_rs)

            @op("v")
            def _(v, base=base, scfb=scfb, m_rs=m_rs, m_sc=m_sc):
                c.wait(v, m_rs)
                v.tensor_scalar(scfb[:], scfb[:], 1e-12, None, ALU.max)
                v.drain()
                v.reciprocal(scfb[:], scfb[:])
                v.drain()
                c.inc(v.tensor_tensor(act3[:, base:base + 1024],
                                      act3[:, base:base + 1024], scfb[:],
                                      ALU.mult), "V", mark=m_sc)

            emit_c3(n)
            if n >= 2:
                k_gate_tp(n - 2)
            if n >= 1:
                k_gate_mm(n - 1)
    NK = 2 * DC
    k_gate_tp(NK - 2)
    k_gate_mm(NK - 1)
    k_gate_tp(NK - 1)
    last["pP1"] = k_rs[NK - 1]
    last["pP2"] = k_te[NK - 1]
    last["pTb"] = k_te[NK - 1]
    last["scb1g"] = k_ss[NK - 1]
    k_sc_last = k_sc[NK - 2]

    def _unused_l2norm(tag, buf, raw_evs, gate_lhs, gate_ready, gate_out,
                       prev_gd=None):
        res = []
        for ec in range(DC):
            for th in range(2):
                idx = ec * 2 + th
                m_sq, m_ss, m_rs, m_sc, m_gm, m_ge = (
                    mk(tag + "Q"), mk(tag + "S"), mk(tag + "R"),
                    mk(tag + "C"), mk(tag + "G"), mk(tag + "H"))
                base = T * ec + 1024 * th
                pfree = last["pP1"]
                last["pP1"] = m_ge

                @op("s")
                def _(s, idx=idx, base=base, m_sq=m_sq,
                      g0=(last["scb1g"] if idx == 0 else None)):
                    c.wait(s, raw_evs[idx])
                    c.wait(s, g0)
                    if idx >= 1:
                        c.wait(s, res[idx - 1][0])
                    c.inc(s.activation(scb1[:], buf[:, base:base + 1024],
                                       AF.Square), "S", mark=m_sq)

                @op("t")
                def _(t, m_sq=m_sq, m_ss=m_ss, pfree=pfree):
                    c.wait(t, m_sq)
                    c.wait(t, "e2:done")
                    c.wait(t, pfree)
                    for nn in range(2):
                        mmi = t.matmul(pP1[:, 512 * nn:512 * (nn + 1)],
                                       E2rep[:],
                                       scb1[:, 512 * nn:512 * (nn + 1)],
                                       start=True, stop=True)
                    c.inc(mmi, "T", mark=m_ss)

                @op("s")
                def _(s, m_ss=m_ss, m_rs=m_rs):
                    c.wait(s, m_ss)
                    c.inc(s.activation(scf1[:], pP1[:], AF.Sqrt), "S",
                          mark=m_rs)

                @op("v")
                def _(v, base=base, m_rs=m_rs, m_sc=m_sc):
                    c.wait(v, m_rs)
                    v.tensor_scalar(scf1[:], scf1[:], 1e-12, None, ALU.max)
                    v.drain()
                    v.reciprocal(scf1[:], scf1[:])
                    v.drain()
                    c.inc(v.tensor_tensor(buf[:, base:base + 1024],
                                          buf[:, base:base + 1024], scf1[:],
                                          ALU.mult), "V", mark=m_sc)

                if gate_out is None:
                    res.append((m_ss, m_sc, m_sc))
                    last["pP1"] = m_sc
                    continue

                @op("t")
                def _(t, base=base, m_sc=m_sc, m_gm=m_gm):
                    c.wait(t, m_sc)
                    c.wait(t, gate_ready)
                    for nn in range(2):
                        mmi = t.matmul(pP1[0:2, 512 * nn:512 * (nn + 1)],
                                       gate_lhs[:, 0:2],
                                       buf[:, base + 512 * nn:
                                           base + 512 * (nn + 1)],
                                       start=True, stop=True)
                    c.inc(mmi, "T", mark=m_gm)

                m_gd = mk(tag + "D")

                @op("v")
                def _(v, ec=ec, th=th, m_gm=m_gm, m_ge=m_ge,
                      prevd=(res[-1][2] if res else prev_gd)):
                    c.wait(v, m_gm)
                    c.wait(v, prevd)
                    c.inc(v.tensor_copy(gstage[0:2, :], pP1[0:2, :]), "V",
                          mark=m_ge)

                @op("g")
                def _(g, ec=ec, th=th, m_ge=m_ge, m_gd=m_gd):
                    c.wait(g, m_ge)
                    c.dma(g, lambda: g.dma_start(
                        out=gate_out[2 * ec:2 * ec + 2,
                                     1024 * th:1024 * (th + 1)],
                        in_=gstage[0:2, :]), "g", mark=m_gd)
                res.append((m_ss, m_ge, m_gd))
        return [r[2] for r in res]

    vp_ev = []
    for tc in range(TC):
        m_vs = mk("c8V")

        @op("v")
        def _(v, tc=tc, m_vs=m_vs):
            c.wait(v, kg_ev[tc // 8])
            c.wait(v, vbev[tc])
            gT = gatekT.ap()
            gb = AP(gT.tensor, gT.offset + 16 * tc,
                    [[gT.ap[0][0], 128], [1, 16], [0, 64]])
            av = act1[:, 1024 * tc:1024 * (tc + 1)].rearrange(
                "p (h q) -> p h q", h=16)
            c.inc(v.tensor_tensor(av, av, gb, ALU.mult), "V", mark=m_vs)
        vp_ev.append(m_vs)
    sfree("gstage")
    sfree("gatekT")

    RB = fmA
    act4 = salloc("act4", [128, HP * 1024], BF16, side="right")
    gqb = Wb0[:, 0:2048]
    stg = Wb0[:, 2048:4096].bitcast(F32)

    qsc = {}
    qn_prev = {"ss": last["scb1g"], "sc": k_sc_last}

    qn_pend = []

    def qnorm_scale():
        while qn_pend:
            idx, base, m_rs, m_sc = qn_pend.pop(0)

            @op("v")
            def _(v, base=base, m_rs=m_rs, m_sc=m_sc):
                c.wait(v, m_rs)
                v.tensor_scalar(scf1[:], scf1[:], 1e-12, None, ALU.max)
                v.drain()
                v.reciprocal(scf1[:], scf1[:])
                v.drain()
                c.inc(v.tensor_tensor(act2[:, base:base + 1024],
                                      act2[:, base:base + 1024], scf1[:],
                                      ALU.mult), "V", mark=m_sc)
            qsc[idx] = m_sc

    def qnorm_inline(idx, split=False):
        ec, thh = idx // 2, idx % 2
        base = T * ec + 1024 * thh
        m_sq, m_ss, m_rs, m_sc = (mk("c7qQ"), mk("c7qS"), mk("c7qR"),
                                  mk("c7qC"))
        ps, pfree = aslot(m_rs)
        pv_ss, pv_sc = qn_prev["ss"], qn_prev["sc"]
        qn_prev["ss"], qn_prev["sc"] = m_ss, m_sc

        @op("v")
        def _(v, base=base, m_sq=m_sq, pv_ss=pv_ss):
            c.wait(v, pv_ss)
            c.inc(v.tensor_tensor(scb1[:], act2[:, base:base + 1024],
                                  act2[:, base:base + 1024], ALU.mult),
                  "V", mark=m_sq)

        @op("t")
        def _(t, ps=ps, m_sq=m_sq, m_ss=m_ss, pfree=pfree):
            c.wait(t, m_sq)
            c.wait(t, "e2:done")
            c.wait(t, pfree)
            for nn in range(2):
                mmi = t.matmul(ps[:, 512 * nn:512 * (nn + 1)], E2rep[:],
                               scb1[:, 512 * nn:512 * (nn + 1)],
                               start=True, stop=True)
            c.inc(mmi, "T", mark=m_ss)

        @op("s")
        def _(s, ps=ps, m_ss=m_ss, m_rs=m_rs, pv_sc=pv_sc):
            c.wait(s, m_ss)
            c.wait(s, pv_sc)
            c.inc(s.activation(scf1[:], ps[:], AF.Sqrt), "S", mark=m_rs)

        qn_pend.append((idx, base, m_rs, m_sc))
        if not split:
            qnorm_scale()

    att_done = []
    prev_cg = {0: None, 1: None}
    c10_st = []
    c10_stm = [mk("cAW") for _ in range(TC)]
    c10_evm = [mk("cAE") for _ in range(TC)]
    lnD_box = [None]
    wo_guard = [None]
    hp_gq_mark = None

    aring = [last["pA"], last["pB"], last["pP1"]]
    apsum = [pA, pB, pP1]
    acnt = [0]

    def aslot(consumer_mark):
        s = acnt[0] % 3
        acnt[0] += 1
        pfree = aring[s]
        aring[s] = consumer_mark
        return apsum[s], pfree

    d_chain = [last["pP2"]]

    def emit_tail(pend):
        h, m_cg1 = pend["h"], pend["cg1"]

        @op("t")
        def _(t, h=h, m_cg1=m_cg1, m_copy=pend["copy"], sgs=pend["sgs"]):
            for m in sgs:
                c.wait(t, m)
            c.wait(t, m_copy)
            c.wait(t, vp_ev[15])
            hb_ = (h % 2) * 64
            for si in range(8):
                sc = 8 + si
                for nn in range(2):
                    mmi = t.matmul(
                        pP2[hb_:hb_ + 64, 512 * nn:512 * (nn + 1)],
                        act1[:, 1024 * sc + 64 * h:1024 * sc + 64 * (h + 1)],
                        RB[:, 1024 * sc + 512 * nn:1024 * sc + 512 * (nn + 1)],
                        start=(si == 0), stop=(si == 7))
            c.inc(mmi, "T", mark=m_cg1)

        @op("v")
        def _(v, h=h, m_cg1=m_cg1, m_ad=pend["ad"], m_ev=pend["ev"],
              m_gq=pend["gq"], wog=pend["wog"]):
            c.wait(v, m_cg1)
            hb_ = (h % 2) * 64
            hp_ = h // 2
            c.inc(v.tensor_tensor(stg[hb_:hb_ + 64, :],
                                  pP2[hb_:hb_ + 64, :],
                                  stg[hb_:hb_ + 64, :], ALU.add), "V",
                  mark=m_ad)
            v.drain()
            c.wait(v, m_gq)
            c.wait(v, wog)
            gslot = 1024 * (hp_ % 2)
            c.inc(v.tensor_tensor(
                act4[hb_:hb_ + 64, 1024 * hp_:1024 * (hp_ + 1)],
                stg[hb_:hb_ + 64, :],
                gqb[hb_:hb_ + 64, gslot:gslot + 1024], ALU.mult),
                "V", mark=m_ev)
        d_chain[0] = pend["ad"]
        att_done.append(pend["ev"])

    pend = None
    for th in range(2):
        qnorm_inline(th)
        pend = None
        for h in range(H):
            hp, hb = h // 2, (h % 2) * 64
            if h % 2 == 0:
                if hp < HP - 1:
                    qnorm_inline(2 * (hp + 1) + th, split=True)
                m_gq, m_gqm = mk("c9GQ"), mk("c9GM")
                ps_g, pfree_g = aslot(m_gq)
                gslot = 1024 * (hp % 2)

                @op("t")
                def _(t, hp=hp, th=th, ps_g=ps_g, m_gqm=m_gqm,
                      pfree_g=pfree_g):
                    c.wait(t, qsc[2 * hp + th])
                    c.wait(t, "gq2rep")
                    c.wait(t, pfree_g)
                    for nn in range(2):
                        mmi = t.matmul(
                            ps_g[:, 512 * nn:512 * (nn + 1)], gq2rep[:],
                            act2[:, T * hp + 1024 * th + 512 * nn:
                                 T * hp + 1024 * th + 512 * (nn + 1)],
                            start=True, stop=True)
                    c.inc(mmi, "T", mark=m_gqm)

                @op("v")
                def _(v, ps_g=ps_g, m_gqm=m_gqm, m_gq=m_gq, gslot=gslot):
                    c.wait(v, m_gqm)
                    c.inc(v.tensor_copy(gqb[:, gslot:gslot + 1024],
                                        ps_g[:]), "V", mark=m_gq)
                hp_gq_mark = m_gq
            else:
                m_gq = hp_gq_mark
                qnorm_scale()

            sg_mm = []

            def emit_align(sc, h=h, th=th):
                m_al, m_sg = mk("c9A"), mk("c9S")
                ps, pfree = aslot(m_sg)
                sg_mm.append(m_sg)
                rbfree = prev_cg[sc // 8]

                @op("t")
                def _(t, h=h, th=th, sc=sc, ps=ps, m_al=m_al, pfree=pfree,
                      first=(h % 2 == 0 and sc == 0)):
                    if first:
                        c.wait(t, qsc[2 * (h // 2) + th])
                    c.wait(t, pfree)
                    hp_, hb_ = h // 2, (h % 2) * 64
                    for nn in range(2):
                        mmi = t.matmul(
                            ps[:, 512 * nn:512 * (nn + 1)],
                            act3[hb_:hb_ + 64,
                                 T * hp_ + 128 * sc:T * hp_ + 128 * (sc + 1)],
                            act2[hb_:hb_ + 64,
                                 T * hp_ + 1024 * th + 512 * nn:
                                 T * hp_ + 1024 * th + 512 * (nn + 1)],
                            start=True, stop=True)
                    c.inc(mmi, "T", mark=m_al)

                @op("s")
                def _(s, sc=sc, ps=ps, m_al=m_al, m_sg=m_sg, rbfree=rbfree):
                    c.wait(s, m_al)
                    c.wait(s, rbfree)
                    c.inc(s.activation(RB[:, 1024 * sc:1024 * (sc + 1)],
                                       ps[:], AF.Sigmoid,
                                       bias=cpool[:, 0:1],
                                       scale=cpool[:, 1:2]), "S",
                          mark=m_sg)

            for sc in range(3):
                emit_align(sc)
            if pend is not None:
                emit_tail(pend)
            for sc in range(3, 11):
                emit_align(sc)

            m_cg0, m_cg1 = mk("c9C0"), mk("c9C1")
            m_copy, m_ad, m_ev = mk("c9P"), mk("c9D"), mk("c9E")
            dfree = d_chain[0]

            @op("t")
            def _(t, h=h, m_cg0=m_cg0, dfree=dfree, sgs=list(sg_mm[0:8])):
                for m in sgs:
                    c.wait(t, m)
                c.wait(t, dfree)
                c.wait(t, vp_ev[7])
                hb_ = (h % 2) * 64
                for si in range(8):
                    for nn in range(2):
                        mmi = t.matmul(
                            pP2[hb_:hb_ + 64, 512 * nn:512 * (nn + 1)],
                            act1[:, 1024 * si + 64 * h:1024 * si + 64 * (h + 1)],
                            RB[:, 1024 * si + 512 * nn:1024 * si + 512 * (nn + 1)],
                            start=(si == 0), stop=(si == 7))
                c.inc(mmi, "T", mark=m_cg0)

            @op("v")
            def _(v, h=h, m_cg0=m_cg0, m_copy=m_copy):
                c.wait(v, m_cg0)
                hb_ = (h % 2) * 64
                c.inc(v.tensor_copy(stg[hb_:hb_ + 64, :],
                                    pP2[hb_:hb_ + 64, :]), "V", mark=m_copy)

            for sc in range(11, 16):
                emit_align(sc)

            prev_cg[0] = m_cg0
            prev_cg[1] = m_cg1
            pend = {"h": h, "cg1": m_cg1, "copy": m_copy, "ad": m_ad,
                    "ev": m_ev, "gq": m_gq, "sgs": list(sg_mm[8:16]),
                    "wog": wo_guard[0]}
        emit_tail(pend)
        pend = None
        last["pA"], last["pB"], last["pP1"] = aring

        if th == 1:
            last["scb1g"] = qn_prev["ss"]
            tp_ping["prev"] = [d_chain[0], d_chain[0]]
            lnD_box[0] = ln_fused(
                "da", dxb0, mv1, inv1, fmA, att_done[-1],
                src_marks=c10_stm, lazy=True,
                stage=[xcb[:, 0:1024], xcb[:, 1024:2048]],
                stage_from=8, stage_marks=c10_evm)
            for _ in range(8):
                lnD_box[0]["emit"]()

        def cw_mms(t, tc, ps, th=th):
            for hp_ in range(HP):
                for nn in range(2):
                    _mmi[0] = t.matmul(
                        ps[:, 512 * nn:512 * (nn + 1)],
                        act4[:, 1024 * hp_ + 128 * (tc % (TC // 2)):
                             1024 * hp_ + 128 * (tc % (TC // 2) + 1)],
                        Wb1[:, 1024 * hp_ + 512 * nn:
                            1024 * hp_ + 512 * (nn + 1)],
                        start=(hp_ == 0), stop=False)
            for nn in range(2):
                _mmi[0] = t.matmul(ps[:, 512 * nn:512 * (nn + 1)],
                                   ones_row[0:1, 0:128],
                                   brow[0:1, 512 * nn:512 * (nn + 1)],
                                   start=False, stop=True)

        prev_half_st = (half_st[-2:] if th == 1 else [])
        half_st = []
        wo_mm_last = None
        for tcl in range(TC // 2):
            tc = th * (TC // 2) + tcl
            ps, pn = pick(tc)
            m_mm, m_ld = mk("cAM"), mk("cAR")
            m_ev = c10_evm[tc]
            m_st = c10_stm[tc]
            pfree = last[pn]
            last[pn] = m_ev
            c10_st.append(m_st)
            half_st.append(m_st)
            wo_mm_last = m_mm

            @op("t")
            def _(t, tc=tc, tcl=tcl, ps=ps, m_mm=m_mm, pfree=pfree, th=th,
                  attw=att_done[-1]):
                if tcl == 0:
                    c.wait(t, attw)
                    c.wait(t, "w:bWo")
                    c.wait(t, "b:bbo")
                c.wait(t, pfree)
                cw_mms(t, tc, ps)
                c.inc(_mmi[0], "T", mark=m_mm)

            @op("y")
            def _(y, tc=tc, tcl=tcl, m_ld=m_ld, th=th,
                  stw=(half_st[tcl - 2] if tcl >= 2 else att_done[-1]),
                  tw=(half_st[tcl - 2] + "t" if (tcl >= 2 and dbg) else None),
                  pw=(prev_half_st[tcl] if (tcl < 2 and prev_half_st)
                      else None)):
                c.wait(y, stw)
                c.wait(y, tw)
                c.wait(y, pw)
                if pw is not None and dbg:
                    c.wait(y, pw + "t")
                c.wait(y, mB_st[tc])
                if dbg:
                    c.wait(y, mB_st[tc] + "t")
                if th == 1 and tcl >= 2:
                    c.wait(y, lnD_box[0]["ap"][tc - 2])
                c.dma(y, lambda: y.dma_start(
                    out=xcb[:, 1024 * (tc % 2):1024 * (tc % 2) + 1024],
                    in_=xrows(dxb1, tc)), "m", mark=m_ld)

            @op("v")
            def _(v, tc=tc, ps=ps, m_mm=m_mm, m_ld=m_ld, m_ev=m_ev):
                c.wait(v, m_mm)
                c.wait(v, m_ld)
                xv = xcb[:, 1024 * (tc % 2):1024 * (tc % 2) + 1024]
                c.inc(v.tensor_tensor(xv, ps[:], xv, ALU.add), "V", mark=m_ev)

            @op("s")
            def _(s, tc=tc, m_ev=m_ev, m_st=m_st):
                c.wait(s, m_ev)
                c.dma(s, lambda: s.dma_start(
                    out=xrows(dxb0, tc),
                    in_=xcb[:, 1024 * (tc % 2):1024 * (tc % 2) + 1024]), "n", mark=m_st)
                if dbg:
                    c.dma(s, lambda: s.dma_start(
                        out=xrows(dtap["t_xc"], tc),
                        in_=xcb[:, 1024 * (tc % 2):1024 * (tc % 2) + 1024]), "n", mark=m_st + "t")
            if th == 1:
                lnD_box[0]["emit"]()
        wo_guard[0] = wo_mm_last
        aring[0] = last["pA"]
        aring[1] = last["pB"]
        aring[2] = last["pP1"]
    mC_st = c10_st[-1]
    last["pP2"] = d_chain[0]
    last["pTb"] = d_chain[0]

    for kk in ("act4", "act1", "act3", "Wb1"):
        sfree(kk)

    lnD = lnD_box[0]
    lnD["flush"]()
    mD_st, _ = ffn_phase("d", dxb0, dy, None, "dO", list(c10_st),
                         att_done[-1], "w:bWo", prev_tap=dbg, lnobj=lnD)

    if c.record:
        pass

    @op("y")
    def _(y):
        for m in mD_st[-2:]:
            c.wait(y, m)

    with nc.Block() as block:
        @block.tensor
        def _(t):
            for fn in work["t"]:
                fn(t)

        @block.vector
        def _(v):
            for fn in work["v"]:
                fn(v)

        @block.scalar
        def _(s):
            for fn in work["s"]:
                fn(s)

        @block.gpsimd
        def _(g):
            for fn in work["g"]:
                fn(g)

        @block.sync
        def _(y):
            for fn in work["y"]:
                fn(y)
            if not c.record:
                for i in range(8):
                    c.wait(y, f"END:dm{i}")
                for i in range(4):
                    c.wait(y, f"END:dn{i}")
                for i in range(80):
                    c.wait(y, f"END:dg{i}")

    if c.record:
        for i in range(8):
            c.tbl[f"END:dm{i}"] = (f"dm{i}", c.cnt[f"dm{i}"])
        for i in range(4):
            c.tbl[f"END:dn{i}"] = (f"dn{i}", c.cnt[f"dn{i}"])
        for i in range(80):
            c.tbl[f"END:dg{i}"] = (f"dg{i}", c.cnt[f"dg{i}"])


def build_core(TC=16, dbg=False):
    tbl = {}
    nc = None
    for pass_i in (0, 1):
        nc = bass.Bass("TRN2", target_bir_lowering=False)
        c = Ctx(nc, record=(pass_i == 0), tbl=tbl)
        _emit(nc, c, TC, dbg)
    return nc


_IN_MAP = [
    ("x", "L0_attn_a_x"),
    ("aWq", "L0_attn_a_Wq"), ("aWk", "L0_attn_a_Wk"), ("aWv", "L0_attn_a_Wv"),
    ("abq", "L0_attn_a_bq"), ("abk", "L0_attn_a_bk"), ("abv", "L0_attn_a_bv"),
    ("bWq", "L0_attn_b_Wq"), ("bWk", "L0_attn_b_Wk"), ("bWv", "L0_attn_b_Wv"),
    ("bWo", "L0_attn_b_Wo"), ("bgq", "L0_attn_b_gq"), ("bgk", "L0_attn_b_gk"),
    ("bbo", "L0_attn_b_bo"), ("blnw", "L0_attn_b_ln_w"),
    ("blnb", "L0_attn_b_ln_b"),
    ("fWg", "L0_ffn_Wg"), ("fA", "L0_ffn_A"), ("fB", "L0_ffn_B"),
]


def kernel(**inputs):
    B = inputs["L0_attn_a_x"].shape[0]
    nc = build_core(TC=inputs["L0_attn_a_x"].shape[1] // 128, dbg=False)
    in_maps = []
    for b in range(B):
        m = {}
        for knm, inm in _IN_MAP:
            a = np.ascontiguousarray(np.asarray(inputs[inm], dtype=np.float32))
            if knm == "x":
                a = np.ascontiguousarray(a[b])
            m[knm] = a
        in_maps.append(m)
    res = run_bass_kernel_spmd(nc, in_maps, core_ids=list(range(B)))
    return np.stack([r["y"] for r in res.results], axis=0)


# version 25
# speedup vs baseline: 1.0080x; 1.0080x over previous
import numpy as np
import concourse.bass as bass
import concourse.mybir as mybir
from concourse.bass_utils import run_bass_kernel_spmd
from concourse.ap import AP

F32 = mybir.dt.float32
BF16 = mybir.dt.bfloat16
AF = mybir.ActivationFunctionType
ALU = mybir.AluOpType
AX = mybir.AxisListType

D = 1024
DC = 8
H = 16
HD = 64
NE = 8
NR = 16
THRESH = 0.29514
SHARP = 15.0
EPS = 1e-5

SEMS = (["T", "V", "S", "G"] + [f"dm{i}" for i in range(8)]
        + [f"dn{i}" for i in range(4)]
        + [f"dg{i}" for i in range(80)])


class Ctx:
    def __init__(self, nc, record, tbl):
        self.nc = nc
        self.record = record
        self.tbl = tbl
        self.sem = {n: nc.alloc_semaphore(n) for n in SEMS}
        self.cnt = {n: 0 for n in SEMS}
        self.dmai = 0
        self.dmni = 0
        self.dmgi = 0

    def inc(self, instr, s, n=1, mark=None):
        if s in ("dW", "dD"):
            s = f"dg{self.dmgi}"
            self.dmgi += 1
            n = 16
        elif s.startswith("d"):
            s = f"dm{self.dmai % 8}"
            self.dmai += 1
            n = 16
        self.cnt[s] += n
        instr.then_inc(self.sem[s], n)
        if mark is not None and self.record:
            assert mark not in self.tbl, mark
            self.tbl[mark] = (s, self.cnt[s])
        return instr

    def wait(self, eng, mark):
        if mark is None or self.record:
            return
        s, v = self.tbl[mark]
        eng.wait_ge(self.sem[s], v)

    def dma(self, eng, fn, pool, mark=None):
        if pool == "g":
            s = f"dg{self.dmgi}"
            self.dmgi += 1
        elif pool == "n":
            s = f"dn{self.dmni % 4}"
            self.dmni += 1
        else:
            s = f"dm{self.dmai % 8}"
            self.dmai += 1
        prevc = self.cnt[s]
        if prevc and not self.record:
            eng.wait_ge(self.sem[s], prevc)
        self.cnt[s] += 16
        instr = fn()
        instr.then_inc(self.sem[s], 16)
        if mark is not None and self.record:
            assert mark not in self.tbl, mark
            self.tbl[mark] = (s, self.cnt[s])
        return instr


def _emit(nc, c, TC, dbg):
    T = TC * 128
    HP = H // 2

    dram = {"x": nc.dram_tensor("x", [T, D], F32, kind="ExternalInput")}
    for nm, shp in [("aWq", [D, D]), ("aWk", [D, D]), ("aWv", [D, D]),
                    ("abq", [D]), ("abk", [D]), ("abv", [D]),
                    ("bWq", [D, D]), ("bWk", [D, D]), ("bWv", [D, D]),
                    ("bWo", [D, D]), ("bgq", [HD]), ("bgk", [HD]),
                    ("bbo", [D]), ("blnw", [D]), ("blnb", [D]),
                    ("fWg", [D, NE]), ("fA", [NE, D, NR]), ("fB", [NE, NR, D])]:
        dram[nm] = nc.dram_tensor(nm, shp, F32, kind="ExternalInput")
    dy = nc.dram_tensor("y", [T, D], F32, kind="ExternalOutput")
    dskew = nc.dram_tensor("skew", [DC * 128 * 2177], BF16, kind="Internal")
    ds3 = nc.dram_tensor("s3", [3 * D], BF16, kind="Internal")
    dxb0 = nc.dram_tensor("xb0", [T, D], F32, kind="Internal")
    dxb1 = nc.dram_tensor("xb1", [T, D], F32, kind="Internal")
    dtap = {}
    if dbg:
        for nm in ("t_xa", "t_xb", "t_xc"):
            dtap[nm] = nc.dram_tensor(nm, [T, D], F32, kind="ExternalOutput")

    def xrows(dt_, tc):
        return dt_.ap()[128 * tc:128 * (tc + 1), :]

    ones_row = nc.alloc_sbuf_tensor([1, 512], BF16)
    ones_col = nc.alloc_sbuf_tensor([128, 1], BF16)
    ident = nc.alloc_sbuf_tensor([128, 128], BF16)
    E2rep = nc.alloc_sbuf_tensor([128, 128], BF16)
    gq2rep = nc.alloc_sbuf_tensor([128, 128], BF16)
    E2C = nc.alloc_sbuf_tensor([128, 2], F32)
    gq2c = nc.alloc_sbuf_tensor([128, 2], BF16)
    gk2 = nc.alloc_sbuf_tensor([128, 2], BF16)
    gcol = nc.alloc_sbuf_tensor([128, 2], F32)
    E8 = nc.alloc_sbuf_tensor([8, 128], BF16)
    cpool = nc.alloc_sbuf_tensor([128, 4], F32)
    mv1 = nc.alloc_sbuf_tensor([128, 2 * TC], F32)
    mv2 = nc.alloc_sbuf_tensor([128, 2 * TC], F32)
    inv1 = nc.alloc_sbuf_tensor([128, TC], F32)
    inv2 = nc.alloc_sbuf_tensor([128, TC], F32)
    st6 = nc.alloc_sbuf_tensor([128, 12], F32)
    nsc = nc.alloc_sbuf_tensor([128, 2 * TC], F32)
    cwb = nc.alloc_sbuf_tensor([128, NE * TC], BF16)
    scf1 = nc.alloc_sbuf_tensor([128, 1024], F32)
    scb1 = nc.alloc_sbuf_tensor([128, 1024], BF16)
    xcb = nc.alloc_sbuf_tensor([128, 2 * 1024], F32)
    xcb2 = nc.alloc_sbuf_tensor([128, 2 * 1024], F32)
    brow = nc.alloc_sbuf_tensor([1, D], BF16)
    Wb0 = nc.alloc_sbuf_tensor([128, DC * 1024], BF16)
    fmA = nc.alloc_sbuf_tensor([128, DC * T], BF16)

    _closers = {}

    def salloc(key, shape, dt_, side="left"):
        g_ = nc.sbuf_tensor(shape, dt_, side=side)
        h = g_.__enter__()
        _closers[key] = g_
        return h

    def sfree(key):
        _closers.pop(key).__exit__(None, None, None)

    pA = nc.alloc_psum_tensor([128, 1024], F32)
    pB = nc.alloc_psum_tensor([128, 1024], F32)
    pP1 = nc.alloc_psum_tensor([128, 1024], F32)
    pP2 = nc.alloc_psum_tensor([128, 1024], F32)
    pTb = AP(pP2.ap().tensor, 0, [[1024, 128], [1, 1024]]).bitcast(BF16)
    gl = xcb[:, 0:NE * TC]
    gsh = xcb[:, NE * TC:2 * NE * TC]
    gih = xcb[:, 2 * NE * TC:3 * NE * TC]
    gm = xcb[:, 3 * NE * TC:3 * NE * TC + 6 * TC]

    work = {"t": [], "v": [], "s": [], "g": [], "y": []}

    def op(e):
        def deco(fn):
            work[e].append(fn)
        return deco

    seq = {"n": 0}

    def mk(tag):
        seq["n"] += 1
        return f"{tag}.{seq['n']}"

    last = {"pA": None, "pB": None, "pP1": None, "pP2": None, "pTb": None,
            "scb1g": None}

    @op("v")
    def _(v):
        v.memset(ones_row[:], 1.0)
        v.memset(ones_col[:], 1.0)
        v.memset(cpool[:, 0:1], -THRESH * SHARP)
        c.inc(v.memset(cpool[:, 1:2], SHARP), "V", mark="set:v")

    @op("g")
    def _(g):
        def fence(instr, m):
            c.inc(instr, "G", mark=m)
            c.wait(g, m)
        fence(g.memset(ident[:], 0.0), "id:ms")
        fence(g.affine_select(out=ident[:], in_=ident[:],
                              compare_op=ALU.not_equal, fill=1.0, base=0,
                              pattern=[[-1, 128]], channel_multiplier=1),
              "id:done")
        s0, s1, s2 = scf1[:, 0:128], scf1[:, 128:256], scf1[:, 256:384]
        fence(g.iota(s0, pattern=[[-64, 2], [0, 64]], base=0,
                     channel_multiplier=1, allow_small_or_imprecise_dtypes=True), "e2:i")
        fence(g.tensor_scalar(s1, s0, 0.0, None, ALU.is_ge), "e2:a")
        fence(g.tensor_scalar(s2, s0, 64.0, None, ALU.is_lt), "e2:b")
        fence(g.tensor_tensor(E2rep[:], s1, s2, ALU.mult), "e2:done")
        fence(g.iota(s0[0:8, :], pattern=[[1, 128]], base=0,
                     channel_multiplier=-16, allow_small_or_imprecise_dtypes=True), "e8:i")
        fence(g.tensor_scalar(s1[0:8, :], s0[0:8, :], 0.0, None,
                              ALU.is_ge), "e8:a")
        fence(g.tensor_scalar(s2[0:8, :], s0[0:8, :], 16.0, None,
                              ALU.is_lt), "e8:b")
        fence(g.tensor_tensor(E8[:], s1[0:8, :], s2[0:8, :], ALU.mult),
              "e8:done")
        fence(g.iota(s0[:, 0:2], pattern=[[-64, 2]], base=0,
                     channel_multiplier=1, allow_small_or_imprecise_dtypes=True), "e2c:i")
        fence(g.tensor_scalar(s1[:, 0:2], s0[:, 0:2], 0.0, None, ALU.is_ge),
              "e2c:a")
        fence(g.tensor_scalar(s2[:, 0:2], s0[:, 0:2], 64.0, None, ALU.is_lt),
              "e2c:b")
        fence(g.tensor_tensor(E2C[:], s1[:, 0:2], s2[:, 0:2], ALU.mult),
              "e2c:done")
        c.dma(g, lambda: g.dma_start(out=gcol[0:64, 0:1],
                          in_=dram["bgq"].ap()[:, None]), "g", mark="g:q0")
        c.dma(g, lambda: g.dma_start(out=gcol[64:128, 0:1],
                          in_=dram["bgq"].ap()[:, None]), "g", mark="g:q1")
        c.dma(g, lambda: g.dma_start(out=gcol[0:64, 1:2],
                          in_=dram["bgk"].ap()[:, None]), "g", mark="g:k0")
        c.dma(g, lambda: g.dma_start(out=gcol[64:128, 1:2],
                          in_=dram["bgk"].ap()[:, None]), "g", mark="g:k1")
        c.wait(g, "g:q0")
        c.wait(g, "g:q1")
        fence(g.tensor_scalar(gq2c[:], E2C[:], gcol[:, 0:1], None, ALU.mult),
              "gq2c")
        fence(g.tensor_scalar(gq2rep[:], E2rep[:], gcol[:, 0:1], None,
                              ALU.mult), "gq2rep")
        c.wait(g, "g:k0")
        c.wait(g, "g:k1")
        fence(g.tensor_scalar(gk2[:], E2C[:], gcol[:, 1:2], None, ALU.mult),
              "setup:g")

    def wload(dst, name, m, waits=()):
        @op("g")
        def _(g):
            for w in waits:
                c.wait(g, w)
            c.dma(g, lambda: g.dma_start(
                out=dst.ap().rearrange("p (ch e) -> p ch e", ch=DC),
                in_=AP(dram[name], 0,
                       [[1024, 128], [128 * 1024, DC], [1, 1024]])), "g", mark=m)

    def bload(slot, name, m, waits=()):
        @op("g")
        def _(g):
            for w in waits:
                c.wait(g, w)
            c.dma(g, lambda: g.dma_start(out=brow[0:1, slot * D:(slot + 1) * D],
                              in_=dram[name].ap()[None, :]), "g", mark=m)

    def batch_inv(tag, mvbuf, invbuf, readymark):
        m_eps, m_sq, m_inv = mk(tag + "E"), mk(tag + "Q"), mk(tag + "I")

        @op("v")
        def _(v):
            c.wait(v, readymark)
            v.drain()
            varap = AP(mvbuf.ap().tensor, mvbuf.ap().offset + 1,
                       [[mvbuf.ap().ap[0][0], 128], [2, TC]])
            c.inc(v.tensor_scalar(nsc[:, 0:TC], varap, EPS, None, ALU.add),
                  "V", mark=m_eps)

        @op("s")
        def _(s):
            c.wait(s, m_eps)
            c.inc(s.activation(nsc[:, TC:2 * TC], nsc[:, 0:TC], AF.Sqrt),
                  "S", mark=m_sq)

        @op("v")
        def _(v):
            c.wait(v, m_sq)
            v.drain()
            c.inc(v.reciprocal(invbuf[:, 0:TC], nsc[:, TC:2 * TC]), "V",
                  mark=m_inv)
        return m_inv

    ln_guard = {"m": None}

    def stream_stats(tag, src, mvbuf, invbuf, src_marks):
        ap_marks = []
        lg = ln_guard["m"]
        for tc in range(TC):
            lm, st = mk(tag + "L"), mk(tag + "S")
            ap_marks.append(st)

            @op("y")
            def _(y, tc=tc, lm=lm, lg=lg):
                if src_marks is not None:
                    c.wait(y, src_marks[tc])
                if tc >= 2:
                    c.wait(y, ap_marks[tc - 2])
                elif lg is not None:
                    c.wait(y, lg)
                c.dma(y, lambda: y.dma_start(
                    out=xcb2[:, 1024 * (tc % 2):1024 * (tc % 2) + 1024],
                    in_=xrows(src, tc)), "m", mark=lm)

            @op("v")
            def _(v, tc=tc, lm=lm, st=st):
                c.wait(v, lm)
                v.drain()
                xv = xcb2[:, 1024 * (tc % 2):1024 * (tc % 2) + 1024]
                v.bn_stats(st6[:, 0:6], xv[:, 0:512])
                v.bn_stats(st6[:, 6:12], xv[:, 512:1024])
                v.drain()
                c.inc(v.bn_aggr(mvbuf[:, 2 * tc:2 * tc + 2], st6[:, 0:12]),
                      "V", mark=st)
        ln_guard["m"] = ap_marks[TC - 1]
        return batch_inv(tag, mvbuf, invbuf, ap_marks[TC - 1])

    tp_ping = {"i": 0, "prev": [None, None]}

    def transposes(tag, tc, srcmark, src_bf, dst_fm, wguard, tstride=None,
                   toff=None, marks=None):
        if tstride is None:
            tstride = T
        if toff is None:
            toff = 128 * tc
        if marks is None:
            marks = (mk(tag + "TM"), mk(tag + "TE"))
        m_mm, m_ev = marks
        half = tp_ping["i"] % 2
        tp_ping["i"] += 1
        prev = tp_ping["prev"][half]
        tp_ping["prev"][half] = m_ev
        last["pTb"] = m_ev
        base = 1024 * half

        @op("t")
        def _(t):
            c.wait(t, srcmark)
            c.wait(t, "id:done")
            c.wait(t, prev)
            for j in range(8):
                mmi = t.transpose(pTb[:, base + 128 * j:base + 128 * (j + 1)],
                                  src_bf[:, 128 * j:128 * (j + 1)], ident[:])
            c.inc(mmi, "T", mark=m_mm)

        @op("v")
        def _(v):
            c.wait(v, m_mm)
            c.wait(v, wguard)
            dap = AP(dst_fm.ap().tensor, dst_fm.ap().offset + toff,
                     [[dst_fm.ap().ap[0][0], 128], [tstride, 8], [1, 128]])
            c.inc(v.tensor_copy(
                dap,
                pTb[:, base:base + 1024].rearrange("p (j q) -> p j q", j=8)),
                "V", mark=m_ev)
        return m_mm, m_ev

    def ln_fused(tag, src, mvbuf, invbuf, dst_fm, fm_guard, src_marks=None,
                 inner=False, lazy=False, stage=None, stage_from=0,
                 stage_marks=None):
        ld_m = [mk(tag + "L") for _ in range(TC)]
        st_m = [mk(tag + "S") for _ in range(TC)]
        qq_m = [mk(tag + "Q") for _ in range(TC)]
        ap_m = [mk(tag + "A") for _ in range(TC)]
        tp_m = [(mk(tag + "TM"), mk(tag + "TE")) for _ in range(TC)]
        s2_m = [mk(tag + "T") for _ in range(TC)] if inner else []
        q2_m = [mk(tag + "U") for _ in range(TC)] if inner else []
        i2_m = [mk(tag + "I") for _ in range(TC)] if inner else []
        ev_list = [m[1] for m in tp_m]
        lg = ln_guard["m"]
        ln_guard["m"] = ap_m[TC - 1]
        g0 = last["scb1g"]
        nctr = [0]

        def emit_chunk():
            tc = nctr[0]
            nctr[0] += 1
            lm, m_st, m_qq, ma = ld_m[tc], st_m[tc], qq_m[tc], ap_m[tc]
            prev_mm = tp_m[tc - 1][0] if tc else None

            staged = stage is not None and tc >= stage_from
            if not staged:
                @op("y")
                def _(y, tc=tc, lm=lm):
                    if src_marks is not None:
                        c.wait(y, src_marks[tc])
                    if tc >= 2:
                        c.wait(y, ap_m[tc - 2])
                    else:
                        c.wait(y, lg)
                    c.dma(y, lambda: y.dma_start(
                        out=xcb2[:, 1024 * (tc % 2):1024 * (tc % 2) + 1024],
                        in_=xrows(src, tc)), "m", mark=lm)

            def xsrc(tc, staged=staged):
                if not staged:
                    return xcb2[:, 1024 * (tc % 2):1024 * (tc % 2) + 1024]
                return stage[tc % len(stage)]

            @op("v")
            def _(v, tc=tc, lm=lm, m_st=m_st, staged=staged):
                if not staged:
                    c.wait(v, lm)
                else:
                    c.wait(v, (stage_marks or src_marks)[tc])
                v.drain()
                xv = xsrc(tc)
                v.bn_stats(st6[:, 0:6], xv[:, 0:512])
                v.bn_stats(st6[:, 6:12], xv[:, 512:1024])
                v.drain()
                v.bn_aggr(mvbuf[:, 2 * tc:2 * tc + 2], st6[:, 0:12])
                v.drain()
                c.inc(v.tensor_scalar(nsc[:, tc:tc + 1],
                                      mvbuf[:, 2 * tc + 1:2 * tc + 2],
                                      EPS, None, ALU.add), "V", mark=m_st)

            @op("s")
            def _(s, tc=tc, m_st=m_st, m_qq=m_qq):
                c.wait(s, m_st)
                c.inc(s.activation(nsc[:, TC + tc:TC + tc + 1],
                                   nsc[:, tc:tc + 1], AF.Sqrt), "S",
                      mark=m_qq)

            if not inner:
                @op("v")
                def _(v, tc=tc, m_qq=m_qq, ma=ma, prev_mm=prev_mm,
                      g0=(g0 if tc == 0 else None)):
                    c.wait(v, m_qq)
                    c.wait(v, g0)
                    c.wait(v, prev_mm)
                    v.reciprocal(invbuf[:, tc:tc + 1],
                                 nsc[:, TC + tc:TC + tc + 1])
                    v.drain()
                    xv = xsrc(tc)
                    c.inc(v.tensor_scalar(scb1[:], xv,
                                          mvbuf[:, 2 * tc:2 * tc + 1],
                                          invbuf[:, tc:tc + 1],
                                          ALU.subtract, ALU.mult), "V",
                          mark=ma)
            else:
                m_s2, m_q2, m_i2 = s2_m[tc], q2_m[tc], i2_m[tc]

                @op("v")
                def _(v, tc=tc, m_qq=m_qq, m_s2=m_s2,
                      g0=(g0 if tc == 0 else None)):
                    c.wait(v, m_qq)
                    c.wait(v, g0)
                    v.reciprocal(invbuf[:, tc:tc + 1],
                                 nsc[:, TC + tc:TC + tc + 1])
                    v.drain()
                    xv = xsrc(tc)
                    v.tensor_scalar(scf1[:], xv,
                                    mvbuf[:, 2 * tc:2 * tc + 1],
                                    invbuf[:, tc:tc + 1], ALU.subtract,
                                    ALU.mult)
                    v.drain()
                    v.bn_stats(st6[:, 0:6], scf1[:, 0:512])
                    v.bn_stats(st6[:, 6:12], scf1[:, 512:1024])
                    v.drain()
                    v.bn_aggr(mv2[:, 2 * tc:2 * tc + 2], st6[:, 0:12])
                    v.drain()
                    c.inc(v.tensor_scalar(nsc[:, tc:tc + 1],
                                          mv2[:, 2 * tc + 1:2 * tc + 2],
                                          EPS, None, ALU.add), "V",
                          mark=m_s2)

                @op("s")
                def _(s, tc=tc, m_s2=m_s2, m_q2=m_q2):
                    c.wait(s, m_s2)
                    c.inc(s.activation(nsc[:, TC + tc:TC + tc + 1],
                                       nsc[:, tc:tc + 1], AF.Sqrt), "S",
                          mark=m_q2)

                @op("v")
                def _(v, tc=tc, m_q2=m_q2, m_i2=m_i2, ma=ma,
                      prev_mm=prev_mm):
                    c.wait(v, m_q2)
                    c.wait(v, prev_mm)
                    c.inc(v.reciprocal(inv2[:, tc:tc + 1],
                                       nsc[:, TC + tc:TC + tc + 1]), "V",
                          mark=m_i2)
                    v.drain()
                    c.inc(v.tensor_copy(scb1[:], scf1[:]), "V", mark=ma)

            fg = fm_guard["m"] if isinstance(fm_guard, dict) else fm_guard
            transposes(tag, tc, ma, scb1, dst_fm,
                       fg if tc == 0 else None, marks=tp_m[tc])
            last["scb1g"] = tp_m[tc][0]

        def flush():
            while nctr[0] < TC:
                emit_chunk()

        obj = {"ev": ev_list, "i2": i2_m, "emit": emit_chunk,
               "flush": flush, "ap": ap_m}
        if not lazy:
            flush()
        if inner:
            return obj if lazy else (ev_list, i2_m)
        return obj if lazy else ev_list

    def ln_pass(tag, src, mvbuf, invbuf, statsmark, dst_fm, fm_guard,
                inner=False, src_marks=None):
        ap_marks = []
        ev_list = []
        last_ev = None
        prev_mm = None
        lg = ln_guard["m"]
        for tc in range(TC):
            lm, ma = mk(tag + "L"), mk(tag + "A")
            ap_marks.append(ma)

            @op("y")
            def _(y, tc=tc, lm=lm, lg=lg):
                if src_marks is not None:
                    c.wait(y, src_marks[tc])
                if tc >= 2:
                    c.wait(y, ap_marks[tc - 2])
                else:
                    c.wait(y, statsmark)
                    c.wait(y, lg)
                c.dma(y, lambda: y.dma_start(
                    out=xcb2[:, 1024 * (tc % 2):1024 * (tc % 2) + 1024],
                    in_=xrows(src, tc)), "m", mark=lm)

            @op("v")
            def _(v, tc=tc, lm=lm, ma=ma, prev_mm=prev_mm,
                  g0=(last["scb1g"] if tc == 0 else None)):
                c.wait(v, lm)
                if tc == 0:
                    c.wait(v, statsmark)
                    c.wait(v, g0)
                c.wait(v, prev_mm)
                xv = xcb2[:, 1024 * (tc % 2):1024 * (tc % 2) + 1024]
                if inner:
                    v.drain()
                    v.tensor_scalar(scf1[:], xv, mvbuf[:, 2 * tc:2 * tc + 1],
                                    invbuf[:, tc:tc + 1], ALU.subtract,
                                    ALU.mult)
                    v.drain()
                    v.bn_stats(st6[:, 0:6], scf1[:, 0:512])
                    v.bn_stats(st6[:, 6:12], scf1[:, 512:1024])
                    v.drain()
                    v.bn_aggr(mv2[:, 2 * tc:2 * tc + 2], st6[:, 0:12])
                    c.inc(v.tensor_copy(scb1[:], scf1[:]), "V", mark=ma)
                else:
                    c.inc(v.tensor_scalar(scb1[:], xv,
                                          mvbuf[:, 2 * tc:2 * tc + 1],
                                          invbuf[:, tc:tc + 1], ALU.subtract,
                                          ALU.mult), "V", mark=ma)
            m_mm, m_ev = transposes(tag, tc, ma, scb1, dst_fm,
                                    fm_guard if tc == 0 else None)
            prev_mm = m_mm
            last["scb1g"] = m_mm
            ev_list.append(m_ev)
            last_ev = m_ev
        ln_guard["m"] = ap_marks[TC - 1]
        if not inner:
            return ev_list
        m_inv2 = batch_inv(tag + "2", mv2, inv2, ap_marks[TC - 1])
        return ev_list, m_inv2

    def xn2_pass(tag, src, m_inv2, dst_fm, fm_guard, lnw_ready, scb2):
        ap_marks = []
        last_ev = None
        prev_mm = None
        for tc in range(TC):
            lm, ma = mk(tag + "L"), mk(tag + "A")
            ap_marks.append(ma)

            @op("y")
            def _(y, tc=tc, lm=lm):
                if tc >= 2:
                    c.wait(y, ap_marks[tc - 2])
                else:
                    c.wait(y, m_inv2)
                c.dma(y, lambda: y.dma_start(
                    out=xcb[:, 1024 * (tc % 2):1024 * (tc % 2) + 1024],
                    in_=xrows(src, tc)), "m", mark=lm)

            @op("v")
            def _(v, tc=tc, lm=lm, ma=ma, prev_mm=prev_mm):
                c.wait(v, lm)
                if tc == 0:
                    c.wait(v, m_inv2)
                    c.wait(v, lnw_ready)
                c.wait(v, prev_mm)
                xv = xcb2[:, 1024 * (tc % 2):1024 * (tc % 2) + 1024]
                v.tensor_scalar(scf1[:], xv, mv1[:, 2 * tc:2 * tc + 1],
                                inv1[:, tc:tc + 1], ALU.subtract, ALU.mult)
                v.drain()
                v.tensor_scalar(scf1[:], scf1[:], mv2[:, 2 * tc:2 * tc + 1],
                                inv2[:, tc:tc + 1], ALU.subtract, ALU.mult)
                v.drain()
                v.tensor_tensor(scf1[:], scf1[:], lnwb[:], ALU.mult)
                v.drain()
                c.inc(v.tensor_tensor(scb2[:], scf1[:], lnbb[:], ALU.add),
                      "V", mark=ma)
            m_mm, m_ev = transposes(tag, tc, ma, scb2, dst_fm,
                                    fm_guard if tc == 0 else None)
            prev_mm = m_mm
            last_ev = m_ev
        return last_ev

    def pick(idx):
        return (pA, "pA") if idx % 2 == 0 else (pB, "pB")

    def tok_proj(tag, lhs_fm, lhs_ready, Wbuf, w_ready, out_sb,
                 bias_slot=None, bias_mark=None):
        ev_marks = []
        for tc in range(TC):
            ps, pn = pick(tc)
            m_mm, m_ev = mk(tag + "M"), mk(tag + "E")
            pfree = last[pn]
            last[pn] = m_ev
            ev_marks.append(m_ev)

            @op("t")
            def _(t, tc=tc, ps=ps, m_mm=m_mm, pfree=pfree):
                if isinstance(lhs_ready, list):
                    c.wait(t, lhs_ready[tc])
                elif tc == 0:
                    c.wait(t, lhs_ready)
                if tc == 0:
                    c.wait(t, w_ready)
                    c.wait(t, "set:v")
                    c.wait(t, bias_mark)
                c.wait(t, pfree)
                for dc in range(DC):
                    for nn in range(2):
                        stop = (bias_slot is None and dc == DC - 1)
                        mmi = t.matmul(
                            ps[:, 512 * nn:512 * (nn + 1)],
                            lhs_fm[:, T * dc + 128 * tc:T * dc + 128 * (tc + 1)],
                            Wbuf[:, 1024 * dc + 512 * nn:
                                 1024 * dc + 512 * (nn + 1)],
                            start=(dc == 0), stop=stop)
                if bias_slot is not None:
                    for nn in range(2):
                        mmi = t.matmul(
                            ps[:, 512 * nn:512 * (nn + 1)],
                            ones_row[0:1, 0:128],
                            brow[0:1, bias_slot * D + 512 * nn:
                                 bias_slot * D + 512 * (nn + 1)],
                            start=False, stop=True)
                c.inc(mmi, "T", mark=m_mm)

            @op("v")
            def _(v, tc=tc, ps=ps, m_mm=m_mm, m_ev=m_ev):
                c.wait(v, m_mm)
                c.inc(v.tensor_copy(out_sb[:, 1024 * tc:1024 * (tc + 1)],
                                    ps[:]), "V", mark=m_ev)
        return ev_marks

    def resid_out(tag, nk, emit_mms, resid_src, resid_dst, store_sem,
                  first_wait=(), tapname=None, load_guard=None, stm=None,
                  hook=None, slots=2, evm=None, xstage=None,
                  hook_apm=None, prev_stage=None, stage_guard=None):
        ev = []
        if stm is None:
            stm = [mk(tag + "W") for _ in range(TC)]
        rot = [(pA, "pA"), (pB, "pB"), (pP1, "pP1")][:slots]
        nst = len(xstage) if xstage else 2

        def xv_of(tc):
            if xstage:
                return xstage[tc % nst]
            return xcb[:, 1024 * (tc % 2):1024 * (tc % 2) + 1024]

        for tc in range(TC):
            ps, pn = rot[tc % slots]
            m_mm, m_ld, m_ev = (mk(tag + "M"), mk(tag + "R"),
                                mk(tag + "E"))
            if evm is not None:
                m_ev = evm[tc]
            m_st = stm[tc]
            pfree = last[pn]
            last[pn] = m_ev
            ev.append(m_ev)

            @op("t")
            def _(t, tc=tc, ps=ps, m_mm=m_mm, pfree=pfree):
                if tc == 0:
                    for w in first_wait:
                        c.wait(t, w)
                c.wait(t, pfree)
                emit_mms(t, tc, ps)
                c.inc(t.last_instr if False else _mmi[0], "T", mark=m_mm)

            @op("y")
            def _(y, tc=tc, m_ld=m_ld):
                if tc >= nst:
                    c.wait(y, stm[tc - nst])
                    if tapname is not None:
                        c.wait(y, stm[tc - nst] + "t")
                    if hook_apm is not None:
                        c.wait(y, hook_apm[tc - nst])
                else:
                    if load_guard is not None:
                        for m in (load_guard
                                  if isinstance(load_guard, (list, tuple))
                                  else (load_guard,)):
                            c.wait(y, m)
                    if prev_stage is not None:
                        for lst in prev_stage:
                            c.wait(y, lst[TC - nst + tc])
                    if tc >= 2 and stage_guard is not None:
                        c.wait(y, stage_guard)
                c.dma(y, lambda: y.dma_start(
                    out=xv_of(tc),
                    in_=xrows(resid_src, tc)), "m", mark=m_ld)

            @op("v")
            def _(v, tc=tc, ps=ps, m_mm=m_mm, m_ld=m_ld, m_ev=m_ev):
                c.wait(v, m_mm)
                c.wait(v, m_ld)
                xv = xv_of(tc)
                c.inc(v.tensor_tensor(xv, ps[:], xv, ALU.add), "V", mark=m_ev)

            @op("s")
            def _(s, tc=tc, m_ev=m_ev, m_st=m_st):
                c.wait(s, m_ev)
                c.dma(s, lambda: s.dma_start(
                    out=xrows(resid_dst, tc),
                    in_=xv_of(tc)), "n", mark=m_st)
                if tapname is not None:
                    c.dma(s, lambda: s.dma_start(
                        out=xrows(dtap[tapname], tc),
                        in_=xv_of(tc)), "n", mark=m_st + "t")
            if hook is not None:
                hook()
        return ev, stm

    _mmi = [None]

    def fm_proj(tag, Wbuf, w_ready, rhs_fm, rhs_ready, out_fm,
                bias_slot=None, bias_mark=None):
        evs = []
        for ec in range(DC):
            for th in range(2):
                idx = ec * 2 + th
                ps, pn = pick(idx)
                m_mm, m_ev = mk(tag + "M"), mk(tag + "E")
                pfree = last[pn]
                last[pn] = m_ev

                @op("t")
                def _(t, ec=ec, th=th, idx=idx, ps=ps, m_mm=m_mm, pfree=pfree):
                    if isinstance(rhs_ready, list):
                        if idx <= 1:
                            c.wait(t, rhs_ready[8 * th + 7])
                    elif idx == 0:
                        c.wait(t, rhs_ready)
                    if idx == 0:
                        c.wait(t, w_ready)
                        c.wait(t, "set:v")
                        c.wait(t, bias_mark)
                    c.wait(t, pfree)
                    for dc in range(DC):
                        for nn in range(2):
                            stop = (bias_slot is None and dc == DC - 1)
                            mmi = t.matmul(
                                ps[:, 512 * nn:512 * (nn + 1)],
                                Wbuf[:, 1024 * dc + 128 * ec:
                                     1024 * dc + 128 * (ec + 1)],
                                rhs_fm[:, T * dc + 1024 * th + 512 * nn:
                                       T * dc + 1024 * th + 512 * (nn + 1)],
                                start=(dc == 0), stop=stop)
                    if bias_slot is not None:
                        bap = AP(brow.ap().tensor,
                                 brow.ap().offset + bias_slot * D + 128 * ec,
                                 [[brow.ap().ap[0][0], 1], [1, 128]])
                        for nn in range(2):
                            mmi = t.matmul(ps[:, 512 * nn:512 * (nn + 1)],
                                           bap, ones_row[0:1, 0:512],
                                           start=False, stop=True)
                    c.inc(mmi, "T", mark=m_mm)

                @op("v")
                def _(v, ec=ec, th=th, ps=ps, m_mm=m_mm, m_ev=m_ev):
                    c.wait(v, m_mm)
                    c.inc(v.tensor_copy(
                        out_fm[:, T * ec + 1024 * th:T * ec + 1024 * (th + 1)],
                        ps[:]), "V", mark=m_ev)
                evs.append(m_ev)
        return evs

    Wb1 = salloc("Wb1", [128, DC * 1024], BF16, side="right")
    act1 = salloc("act1", [128, TC * 1024], BF16, side="right")
    act3 = salloc("act3", [128, TC * 1024], BF16, side="right")
    wload(Wb0, "aWk", "w:aWk")
    wload(Wb1, "aWv", "w:aWv")
    bload(0, "abk", "b:abk")

    mA_fm = ln_fused("a2", dram["x"], mv1, inv1, fmA, None)
    kev = tok_proj("a3", fmA, mA_fm, Wb0, "w:aWk", act1,
                   bias_slot=0, bias_mark="b:abk")
    bload(0, "abv", "b:abv", waits=(kev[TC - 1],))
    vev = tok_proj("a4", fmA, mA_fm, Wb1, "w:aWv", act3,
                   bias_slot=0, bias_mark="b:abv")
    wload(Wb0, "aWq", "w:aWq", waits=(kev[TC - 1],))
    bload(0, "abq", "b:abq", waits=(vev[TC - 1],))

    srow = salloc("srow", [1, D], BF16, side="right")
    gch = salloc("gch", [128, 2 * 1024], BF16, side="right")
    rdb = salloc("rdb", [128, 4 * 1024], BF16, side="right")

    g_ev, g_wr = [], []
    for ic in range(DC):
        ps, pn = pick(ic)
        m_mm, m_ev, m_wr = mk("a5M"), mk("a5E"), mk("a5W")
        pfree = last[pn]
        last[pn] = m_ev

        @op("t")
        def _(t, ic=ic, ps=ps, m_mm=m_mm, pfree=pfree, act1=act1,
              act3=act3):
            if ic == 0:
                c.wait(t, kev[TC - 1])
                c.wait(t, vev[TC - 1])
            c.wait(t, pfree)
            for tc in range(TC):
                for nn in range(2):
                    mmi = t.matmul(
                        ps[:, 512 * nn:512 * (nn + 1)],
                        act1[:, 1024 * tc + 128 * ic:1024 * tc + 128 * (ic + 1)],
                        act3[:, 1024 * tc + 512 * nn:1024 * tc + 512 * (nn + 1)],
                        start=(tc == 0), stop=(tc == TC - 1))
            c.inc(mmi, "T", mark=m_mm)

        @op("v")
        def _(v, ic=ic, ps=ps, m_mm=m_mm, m_ev=m_ev):
            c.wait(v, m_mm)
            if ic >= 2:
                c.wait(v, g_wr[ic - 2])
                c.wait(v, g_wr[ic - 2] + "a")
            c.inc(v.tensor_copy(gch[:, 1024 * (ic % 2):1024 * (ic % 2) + 1024],
                                ps[:]), "V", mark=m_ev)

        @op("s")
        def _(s, ic=ic, m_ev=m_ev, m_wr=m_wr):
            c.wait(s, m_ev)
            base = ic * (128 * 2177)
            gv = gch[:, 1024 * (ic % 2):1024 * (ic % 2) + 1024]
            c.dma(s, lambda: s.dma_start(out=AP(dskew, base, [[2177, 128], [1, 1024]]),
                              in_=gv), "n", mark=m_wr + "a")
            c.dma(s, lambda: s.dma_start(out=AP(dskew, base + 1024,
                                     [[2177, 128], [1, 1024]]), in_=gv), "n", mark=m_wr)
        g_ev.append(m_ev)
        g_wr.append(m_wr)

    sfree("rdb")
    sfree("gch")
    sfree("srow")
    sfree("act3")
    sfree("act1")
    Qcirc = salloc("Qcirc", [128, 2 * D], BF16, side="right")
    act2 = salloc("act2", [128, TC * 1024], BF16)

    qev = fm_proj("a7", Wb0, "w:aWq", fmA, mA_fm, act2,
                  bias_slot=0, bias_mark="b:abq")

    sa_marks = []
    for ic in range(DC):
        m_rd, m_ac = mk("a6R"), mk("a6A")
        sa_marks.append(m_ac)

        @op("y")
        def _(y, ic=ic, m_rd=m_rd):
            c.wait(y, g_wr[ic])
            c.wait(y, g_wr[ic] + "a")
            if ic >= 4:
                c.wait(y, sa_marks[ic - 4])
            base = ic * (128 * 2177) + 1024
            off = 128 * ic
            sl = rdb[:, 1024 * (ic % 4):1024 * (ic % 4) + 1024]
            if off == 0:
                c.dma(y, lambda: y.dma_start(out=sl,
                                  in_=AP(dskew, base, [[2176, 128], [1, 1024]])), "m", mark=m_rd)
            else:
                c.dma(y, lambda: y.dma_start(out=sl[:, off:1024],
                                   in_=AP(dskew, base,
                                          [[2176, 128], [1, 1024 - off]])), "m", mark=m_rd + "a")
                c.dma(y, lambda: y.dma_start(out=sl[:, 0:off],
                                  in_=AP(dskew, base + 1024 - off,
                                         [[2176, 128], [1, off]])), "m", mark=m_rd)

        @op("v")
        def _(v, ic=ic, m_rd=m_rd, m_ac=m_ac):
            c.wait(v, m_rd)
            if ic > 0:
                c.wait(v, m_rd + "a")
            sl = rdb[:, 1024 * (ic % 4):1024 * (ic % 4) + 1024]
            if ic == 0:
                c.wait(v, "setup:g")
                c.inc(v.tensor_copy(scf1[:], sl), "V", mark=m_ac)
            else:
                v.drain()
                c.inc(v.tensor_tensor(scf1[:], scf1[:], sl, ALU.add), "V",
                      mark=m_ac)

    @op("v")
    def _(v):
        c.wait(v, sa_marks[DC - 1])
        v.drain()
        c.inc(v.tensor_copy(scb1[:], scf1[:]), "V", mark="a6C")

    m_pP1free = last["pP1"]
    last["pP1"] = "a6srow"

    @op("t")
    def _(t):
        c.wait(t, "a6C")
        c.wait(t, m_pP1free)
        t.matmul(pP1[0:1, 0:512], ones_col[:], scb1[:, 0:512],
                 start=True, stop=True)
        c.inc(t.matmul(pP1[0:1, 512:1024], ones_col[:], scb1[:, 512:1024],
                       start=True, stop=True), "T", mark="a6S")

    @op("s")
    def _(s):
        c.wait(s, "a6S")
        c.inc(s.activation(srow[0:1, 0:D], pP1[0:1, 0:D], AF.Copy), "S",
              mark="a6srow")

    @op("y")
    def _(y):
        c.wait(y, "a6srow")
        for r in range(3):
            c.dma(y, lambda r=r: y.dma_start(out=AP(ds3, r * D, [[D, 1], [1, D]]),
                              in_=srow[0:1, 0:D]), "m", mark=f"a6s3:{r}")
        for r in range(3):
            c.wait(y, f"a6s3:{r}")
        c.dma(y, lambda: y.dma_start(out=Qcirc[:], in_=AP(ds3, 0, [[1, 128], [1, 2048]])), "m", mark="a6Q")

    def a8_mms(t, tc, ps):
        for a in range(DC):
            for nn in range(2):
                _mmi[0] = t.matmul(
                    ps[:, 512 * nn:512 * (nn + 1)],
                    act2[:, T * a + 128 * tc:T * a + 128 * (tc + 1)],
                    Qcirc[:, 128 * a + 512 * nn:128 * a + 512 * (nn + 1)],
                    start=(a == 0), stop=(a == DC - 1))

    a8_stm = [mk("a8W") for _ in range(TC)]
    a8_evm = [mk("a8E") for _ in range(TC)]
    xs4 = [xcb[:, 0:1024], xcb[:, 1024:2048],
           xcb2[:, 0:1024], xcb2[:, 1024:2048]]
    xg_a = ln_guard["m"]
    lnB = ln_fused("ba", dxb0, mv1, inv1, fmA, qev[15], src_marks=a8_evm,
                   lazy=True, stage=xs4)
    a8_ev, a8_st = resid_out("a8", DC, a8_mms, dram["x"], dxb0, "dB",
                             first_wait=(qev[15], "a6Q"),
                             tapname="t_xa" if dbg else None,
                             load_guard=qev[15], stm=a8_stm, evm=a8_evm,
                             hook=lnB["emit"], slots=3, xstage=xs4,
                             hook_apm=lnB["ap"], stage_guard=xg_a)
    lnB["flush"]()

    sfree("Qcirc")

    def ffn_phase(tag, src, dst, tapname, store_sem, prev_store, fm_guard,
                  wguard, prev_tap=False, lnobj=None, out_stm=None,
                  out_hook=None, out_ln_guard=None, out_kw=None):
        wgb = salloc(tag + "wgb", [128, DC * NE], BF16, side="right")
        A_all = salloc(tag + "Aall", [128, DC * 128], BF16, side="right")
        B_all = salloc(tag + "Ball", [128, D], BF16, side="right")
        cwT = salloc(tag + "cwT", [8, T], BF16, side="right")
        cwx = salloc(tag + "cwx", [128, T], BF16, side="right")

        @op("g")
        def _(g):
            for m in (prev_store[-2:] if prev_store else ()):
                c.wait(g, m)
            c.dma(g, lambda: g.dma_start(
                out=wgb.ap().rearrange("p (ch e) -> p ch e", ch=DC),
                in_=AP(dram["fWg"], 0, [[NE, 128], [128 * NE, DC], [1, NE]])), "g", mark=tag + "w:fWg")
            for ch in range(DC):
                c.dma(g, lambda: g.dma_start(
                    out=AP(A_all.ap().tensor, 128 * ch,
                           [[A_all.ap().ap[0][0], 128], [NR, NE], [1, NR]]),
                    in_=AP(dram["fA"], 128 * NR * ch,
                           [[NR, 128], [D * NR, NE], [1, NR]])), "g", mark=tag + f"w:fA{ch}")
            c.dma(g, lambda: g.dma_start(out=B_all[:],
                              in_=AP(dram["fB"], 0, [[D, 128], [1, D]])), "g", mark=tag + "w:fB")

        if lnobj is None:
            m_fm = ln_fused(tag + "a", src, mv1, inv1, fmA, fm_guard,
                            src_marks=prev_store)
        else:
            m_fm = lnobj["ev"]
        lg_ev = []
        for tc in range(TC):
            ps, pn = pick(tc)
            m_mm, m_ev = mk(tag + "gM"), mk(tag + "gE")
            pfree = last[pn]
            last[pn] = m_ev

            @op("t")
            def _(t, tc=tc, ps=ps, m_mm=m_mm, pfree=pfree):
                c.wait(t, m_fm[tc])
                if tc == 0:
                    c.wait(t, tag + "w:fWg")
                c.wait(t, pfree)
                for dc in range(DC):
                    mmi = t.matmul(
                        ps[:, 0:NE],
                        fmA[:, T * dc + 128 * tc:T * dc + 128 * (tc + 1)],
                        wgb[:, NE * dc:NE * (dc + 1)],
                        start=(dc == 0), stop=(dc == DC - 1))
                c.inc(mmi, "T", mark=m_mm)

            @op("v")
            def _(v, tc=tc, ps=ps, m_mm=m_mm, m_ev=m_ev):
                c.wait(v, m_mm)
                if tc == 0:
                    c.wait(v, "setup:g")
                    for m in (prev_store[-2:] if prev_store else ()):
                        c.wait(v, m)
                        if prev_tap:
                            c.wait(v, m + "t")
                c.inc(v.tensor_copy(gl[:, NE * tc:NE * (tc + 1)],
                                    ps[:, 0:NE]), "V", mark=m_ev)
            lg_ev.append(m_ev)

        m_sh, m_ex, m_ind, m_em2, m_cwf = (mk(tag + "sh"), mk(tag + "ex"),
                                           mk(tag + "in"), mk(tag + "em"),
                                           mk(tag + "cw"))

        @op("v")
        def _(v):
            c.wait(v, lg_ev[TC - 1])
            v.drain()
            glv = gl.rearrange("p (t e) -> p t e", e=NE)
            v.reduce_max(gm[:, 0:TC], glv, axis=AX.X)
            v.drain()
            m1b = AP(gm.tensor, gm.offset,
                     [[gm.ap[0][0], 128], [1, TC], [0, NE]])
            c.inc(v.tensor_tensor(
                gsh.rearrange("p (t e) -> p t e", e=NE), glv, m1b,
                ALU.subtract), "V", mark=m_sh)

        @op("s")
        def _(s):
            c.wait(s, m_sh)
            c.inc(s.activation(gl[:, 0:NE * TC], gsh[:, 0:NE * TC], AF.Exp),
                  "S", mark=m_ex)

        @op("v")
        def _(v):
            c.wait(v, m_ex)
            v.tensor_scalar(gih[:, 0:NE * TC], gsh[:, 0:NE * TC], 0.0, None,
                            ALU.is_ge)
            v.drain()
            v.scalar_tensor_tensor(gih[:, 0:NE * TC], gih[:, 0:NE * TC],
                                   -1e30, gsh[:, 0:NE * TC], ALU.mult,
                                   ALU.add)
            v.drain()
            v.reduce_max(gm[:, TC:2 * TC],
                         gih.rearrange("p (t e) -> p t e", e=NE),
                         axis=AX.X)
            v.drain()
            m2b = AP(gm.tensor, gm.offset + TC,
                     [[gm.ap[0][0], 128], [1, TC], [0, NE]])
            c.inc(v.tensor_tensor(
                gih.rearrange("p (t e) -> p t e", e=NE),
                gsh.rearrange("p (t e) -> p t e", e=NE), m2b,
                ALU.is_ge), "V", mark=m_ind)

        @op("s")
        def _(s):
            c.wait(s, m_ind)
            c.inc(s.activation(gm[:, 2 * TC:3 * TC], gm[:, TC:2 * TC],
                               AF.Exp), "S", mark=m_em2)

        @op("v")
        def _(v):
            c.wait(v, m_em2)
            v.tensor_scalar(gm[:, 3 * TC:4 * TC], gm[:, 2 * TC:3 * TC], 1.0,
                            None, ALU.add)
            v.drain()
            v.reciprocal(gm[:, 4 * TC:5 * TC], gm[:, 3 * TC:4 * TC])
            v.drain()
            dinvb = AP(gm.tensor, gm.offset + 4 * TC,
                       [[gm.ap[0][0], 128], [1, TC], [0, NE]])
            v.tensor_tensor(gl.rearrange("p (t e) -> p t e", e=NE),
                            gl.rearrange("p (t e) -> p t e", e=NE),
                            dinvb, ALU.mult)
            v.drain()
            c.inc(v.tensor_tensor(cwb[:, 0:NE * TC], gl[:, 0:NE * TC],
                                  gih[:, 0:NE * TC], ALU.mult), "V",
                  mark=m_cwf)

        m_hm = []
        hp_guard = [last["pP1"], last["pB"]]
        for th in range(2):
            m_mm = mk(tag + "hM")
            m_hm.append(m_mm)
            hps = pP1 if th == 0 else pB

            @op("t")
            def _(t, th=th, hps=hps, m_mm=m_mm, pfree=hp_guard[th]):
                if th == 0:
                    for ch in range(DC):
                        c.wait(t, tag + f"w:fA{ch}")
                c.wait(t, m_fm[8 * th + 7])
                c.wait(t, pfree)
                for dc in range(DC):
                    for nn in range(2):
                        mmi = t.matmul(
                            hps[:, 512 * nn:512 * (nn + 1)],
                            A_all[:, 128 * dc:128 * (dc + 1)],
                            fmA[:, T * dc + 1024 * th + 512 * nn:
                                T * dc + 1024 * th + 512 * (nn + 1)],
                            start=(dc == 0), stop=(dc == DC - 1))
                c.inc(mmi, "T", mark=m_mm)

        ct_last = None
        for tc in range(TC):
            m_tm, m_te = mk(tag + "cM"), mk(tag + "cE")

            half = tp_ping["i"] % 2
            tp_ping["i"] += 1
            prev = tp_ping["prev"][half]
            tp_ping["prev"][half] = m_te
            base = 1024 * half

            @op("t")
            def _(t, tc=tc, m_tm=m_tm, prev=prev, base=base):
                if tc == 0:
                    c.wait(t, m_cwf)
                    c.wait(t, "id:done")
                c.wait(t, prev)
                c.inc(t.transpose(pTb[0:NE, base:base + 128],
                                  cwb[:, NE * tc:NE * (tc + 1)], ident[:]),
                      "T", mark=m_tm)

            @op("v")
            def _(v, tc=tc, m_tm=m_tm, m_te=m_te, base=base):
                c.wait(v, m_tm)
                c.inc(v.tensor_copy(cwT[0:NE, 128 * tc:128 * (tc + 1)],
                                    pTb[0:NE, base:base + 128]), "V",
                      mark=m_te)
            ct_last = m_te

        cx_ev = []
        for half in range(2):
            m_mm, m_ev = mk(tag + "xM"), mk(tag + "xE")
            pfree = last["pA"]
            last["pA"] = m_ev

            @op("t")
            def _(t, half=half, m_mm=m_mm, pfree=pfree):
                if half == 0:
                    c.wait(t, ct_last)
                    c.wait(t, "e8:done")
                c.wait(t, pfree)
                for nn in range(2):
                    mmi = t.matmul(pA[:, 512 * nn:512 * (nn + 1)], E8[:],
                                   cwT[0:NE, 1024 * half + 512 * nn:
                                       1024 * half + 512 * (nn + 1)],
                                   start=True, stop=True)
                c.inc(mmi, "T", mark=m_mm)

            @op("v")
            def _(v, half=half, m_mm=m_mm, m_ev=m_ev):
                c.wait(v, m_mm)
                c.inc(v.tensor_copy(cwx[:, 1024 * half:1024 * (half + 1)],
                                    pA[:]), "V", mark=m_ev)
            cx_ev.append(m_ev)

        hw_ev = []
        for th in range(2):
            m_ev = mk(tag + "hE")
            hps = pP1 if th == 0 else pB
            if th == 0:
                last["pP1"] = m_ev
            else:
                last["pB"] = m_ev

            @op("v")
            def _(v, th=th, hps=hps, m_ev=m_ev):
                c.wait(v, m_hm[th])
                c.wait(v, cx_ev[th])
                c.inc(v.tensor_tensor(act2[:, 1024 * th:1024 * (th + 1)],
                                      hps[:], cwx[:, 1024 * th:1024 * (th + 1)],
                                      ALU.mult), "V", mark=m_ev)
            hw_ev.append(m_ev)

        def o_mms(t, tc, ps):
            for nn in range(2):
                _mmi[0] = t.matmul(ps[:, 512 * nn:512 * (nn + 1)],
                                   act2[:, 128 * tc:128 * (tc + 1)],
                                   B_all[:, 512 * nn:512 * (nn + 1)],
                                   start=True, stop=True)

        oguard = [hw_ev[1]] + (prev_store[-2:] if prev_store else [])
        if prev_tap:
            oguard += [m + "t" for m in prev_store[-2:]]
        if out_ln_guard is not None:
            out_ln_guard["m"] = hw_ev[1]
        o_ev, o_st = resid_out(tag + "o", 1, o_mms, src, dst, store_sem,
                               first_wait=(hw_ev[1], tag + "w:fB"),
                               tapname=tapname, load_guard=tuple(oguard),
                               stm=out_stm, hook=out_hook,
                               slots=(3 if out_hook is not None else 2),
                               **(out_kw or {}))
        sfree(tag + "cwx")
        sfree(tag + "cwT")
        sfree(tag + "Ball")
        sfree(tag + "Aall")
        sfree(tag + "wgb")
        return list(o_st), hw_ev[1]

    b_stm = [mk("boW") for _ in range(TC)]
    b_evm = [mk("boE") for _ in range(TC)]
    lnC_guard = {"m": None}
    xg_b = ln_guard["m"]
    lnC = ln_fused("c2", dxb1, mv1, inv1, fmA, lnC_guard, src_marks=b_evm,
                   inner=True, lazy=True, stage=xs4)
    mB_st, mB_fmlast = ffn_phase("b", dxb0, dxb1, "t_xb" if dbg else None,
                                 "dB", list(a8_st), qev[15], "w:aWv",
                                 prev_tap=dbg, lnobj=lnB, out_stm=b_stm,
                                 out_hook=lnC["emit"],
                                 out_ln_guard=lnC_guard,
                                 out_kw=dict(evm=b_evm, xstage=xs4,
                                             hook_apm=lnC["ap"],
                                             prev_stage=(a8_stm, lnB["ap"]),
                                             stage_guard=xg_b))
    lnC["flush"]()

    act3 = salloc("act3", [128, TC * 1024], BF16, side="right")
    fmBq = salloc("fmBq", [128, DC * 512], BF16, side="right")
    lnwb = salloc("lnwb", [128, D], F32, side="right")
    lnbb = salloc("lnbb", [128, D], F32, side="right")
    wload(Wb0, "bWq", "w:bWq", waits=(qev[15],))

    @op("g")
    def _(g):
        c.dma(g, lambda: g.dma_start(out=lnwb[:],
                          in_=AP(dram["blnw"], 0, [[0, 128], [1, D]])), "g", mark="c:lnwb")
        c.dma(g, lambda: g.dma_start(out=lnbb[:],
                          in_=AP(dram["blnb"], 0, [[0, 128], [1, D]])), "g", mark="c:lnbb")

    mC_fm, mC_i2 = lnC["ev"], lnC["i2"]

    def xn2_quarter(tag, qi, fm_guard, prev_ap):
        ap_marks = []
        prev_mm = None
        last_ev = None
        for tcl in range(TC // 4):
            tc = (TC // 4) * qi + tcl
            lm, ma = mk(tag + "L"), mk(tag + "A")
            ap_marks.append(ma)

            @op("y")
            def _(y, tc=tc, tcl=tcl, lm=lm):
                c.wait(y, mB_st[tc])
                if dbg:
                    c.wait(y, mB_st[tc] + "t")
                if tcl >= 2:
                    c.wait(y, ap_marks[tcl - 2])
                else:
                    c.wait(y, prev_ap)
                    c.wait(y, mB_st[14 + (tc % 2)])
                    if dbg:
                        c.wait(y, mB_st[14 + (tc % 2)] + "t")
                c.dma(y, lambda: y.dma_start(
                    out=xcb2[:, 1024 * (tc % 2):1024 * (tc % 2) + 1024],
                    in_=xrows(dxb1, tc)), "m", mark=lm)

            @op("v")
            def _(v, tc=tc, tcl=tcl, lm=lm, ma=ma, prev_mm=prev_mm,
                  g0=(last["scb1g"] if tcl == 0 else None)):
                c.wait(v, lm)
                c.wait(v, mC_i2[tc])
                if tcl == 0:
                    c.wait(v, "c:lnwb")
                    c.wait(v, "c:lnbb")
                    c.wait(v, g0)
                c.wait(v, prev_mm)
                xv = xcb2[:, 1024 * (tc % 2):1024 * (tc % 2) + 1024]
                v.tensor_scalar(scf1[:], xv, mv1[:, 2 * tc:2 * tc + 1],
                                inv1[:, tc:tc + 1], ALU.subtract, ALU.mult)
                v.drain()
                v.tensor_scalar(scf1[:], scf1[:], mv2[:, 2 * tc:2 * tc + 1],
                                inv2[:, tc:tc + 1], ALU.subtract, ALU.mult)
                v.drain()
                v.tensor_tensor(scf1[:], scf1[:], lnwb[:], ALU.mult)
                v.drain()
                c.inc(v.tensor_tensor(scb1[:], scf1[:], lnbb[:], ALU.add),
                      "V", mark=ma)
            m_mm, m_ev = transposes(tag, tc, ma, scb1, fmBq,
                                    fm_guard if tcl == 0 else None,
                                    tstride=512, toff=128 * tcl)
            prev_mm = m_mm
            last["scb1g"] = m_mm
            last_ev = m_ev
        return last_ev, ap_marks[-1]

    def fm_proj_quarter(tag, Wbuf, w_ready, rhs_ready, out_fm, toff,
                        extra_first=()):
        evs, mms = [], []
        for ec in range(DC):
            ps, pn = pick(ec)
            m_mm, m_ev = mk(tag + "M"), mk(tag + "E")
            pfree = last[pn]
            last[pn] = m_ev

            @op("t")
            def _(t, ec=ec, ps=ps, m_mm=m_mm, pfree=pfree):
                if ec == 0:
                    c.wait(t, w_ready)
                    c.wait(t, rhs_ready)
                    for w in extra_first:
                        c.wait(t, w)
                c.wait(t, pfree)
                for dc in range(DC):
                    mmi = t.matmul(
                        ps[:, 0:512],
                        Wbuf[:, 1024 * dc + 128 * ec:1024 * dc + 128 * (ec + 1)],
                        fmBq[:, 512 * dc:512 * (dc + 1)],
                        start=(dc == 0), stop=(dc == DC - 1))
                c.inc(mmi, "T", mark=m_mm)

            @op("v")
            def _(v, ec=ec, ps=ps, m_mm=m_mm, m_ev=m_ev):
                c.wait(v, m_mm)
                c.inc(v.tensor_copy(out_fm[:, T * ec + toff:T * ec + toff + 512],
                                    ps[:, 0:512]), "V", mark=m_ev)
            evs.append(m_ev)
            mms.append(m_mm)
        return evs, mms

    qbev, kbev = [], []
    fmBq_guard = None
    prev_ap = mC_fm[-1]
    for qi in range(4):
        m_x2, prev_ap = xn2_quarter(f"c4q{qi}", qi, fmBq_guard, prev_ap)
        evs, mms = fm_proj_quarter(f"c5q{qi}", Wb0, "w:bWq", m_x2, act2,
                                   512 * qi)
        qbev += evs
        fmBq_guard = mms[-1]
    q_guard = fmBq_guard
    wload(Wb1, "bWk", "w:bWk", waits=(vev[TC - 1],))
    for qi in range(4):
        m_x2, prev_ap = xn2_quarter(f"c4k{qi}", qi, fmBq_guard, prev_ap)
        evs, mms = fm_proj_quarter(f"c6q{qi}", Wb1, "w:bWk", m_x2, act3,
                                   512 * qi)
        kbev += evs
        fmBq_guard = mms[-1]
    wload(Wb0, "bWv", "w:bWv", waits=(q_guard,))

    sfree("lnbb")
    sfree("lnwb")
    sfree("fmBq")
    act1 = salloc("act1", [128, TC * 1024], BF16, side="right")
    vbev = [mk("c3E") for _ in range(TC)]

    def emit_c3(tc):
        ps, pn = pick(tc)
        m_mm, m_ev = mk("c3M"), vbev[tc]
        pfree = last[pn]
        last[pn] = m_ev

        @op("t")
        def _(t, tc=tc, ps=ps, m_mm=m_mm, pfree=pfree):
            c.wait(t, mC_fm[tc])
            if tc == 0:
                c.wait(t, "w:bWv")
                c.wait(t, "set:v")
            c.wait(t, pfree)
            for dc in range(DC):
                for nn in range(2):
                    mmi = t.matmul(
                        ps[:, 512 * nn:512 * (nn + 1)],
                        fmA[:, T * dc + 128 * tc:T * dc + 128 * (tc + 1)],
                        Wb0[:, 1024 * dc + 512 * nn:1024 * dc + 512 * (nn + 1)],
                        start=(dc == 0), stop=(dc == DC - 1))
            c.inc(mmi, "T", mark=m_mm)

        @op("v")
        def _(v, tc=tc, ps=ps, m_mm=m_mm, m_ev=m_ev):
            c.wait(v, m_mm)
            c.inc(v.tensor_copy(act1[:, 1024 * tc:1024 * (tc + 1)],
                                ps[:]), "V", mark=m_ev)

    wload(Wb1, "bWo", "w:bWo", waits=(fmBq_guard,))
    bload(0, "bbo", "b:bbo", waits=(qev[15],))

    gatekT = salloc("gatekT", [128, 16 * TC], F32, side="right")
    gstage = salloc("gstage", [34, 1024], BF16, side="right")
    kping = xcb2[:, 0:1024]

    kg_ev = [None, None]
    k_sq = {}; k_ss = {}; k_rs = {}; k_sc = {}
    k_gm = {}; k_ge = {}; k_tp = {}; k_te = {}
    k_base = {}
    g0_scb = last["scb1g"]
    pP1_0 = last["pP1"]
    gps_0 = [last["pA"], last["pB"]]
    ptb_0 = [tp_ping["prev"][0], tp_ping["prev"][1]]

    def k_gate_mm(n):
        m_gm = mk("c7kG")
        k_gm[n] = m_gm

        @op("t")
        def _(t, n=n, m_gm=m_gm,
              gfree=(None if n else (last["pP2"], last["pTb"]))):
            c.wait(t, k_sc[n])
            if n == 0:
                c.wait(t, "setup:g")
                for m in gfree:
                    c.wait(t, m)
            else:
                c.wait(t, k_te.get(n - 1))
            base = k_base[n]
            for nn in range(2):
                mmi = t.matmul(pP2[0:2, 512 * nn:512 * (nn + 1)],
                               gk2[:, 0:2],
                               act3[:, base + 512 * nn:
                                    base + 512 * (nn + 1)],
                               start=True, stop=True)
            c.inc(mmi, "T", mark=m_gm)

        m_ge = mk("c7kH")
        k_ge[n] = m_ge

        @op("v")
        def _(v, n=n, m_ge=m_ge):
            c.wait(v, k_gm[n])
            c.wait(v, k_tp.get(n - 2))
            gs = 32 * (n % 2)
            c.inc(v.tensor_copy(gstage[gs:gs + 2, :], pP2[0:2, :]), "V",
                  mark=m_ge)

    def k_gate_tp(n):
        m_tp, m_te = mk("c7kT"), mk("c7kE")
        k_tp[n] = m_tp
        k_te[n] = m_te
        half = n % 2
        tb = 1024 * half
        tb_prev = k_te.get(n - 2, ptb_0[half])
        tp_ping["prev"][half] = m_te
        th_, ec = n // DC, n % DC

        @op("t")
        def _(t, n=n, tb=tb, m_tp=m_tp, tb_prev=tb_prev):
            c.wait(t, k_ge[n])
            if n == 0:
                c.wait(t, "id:done")
            c.wait(t, tb_prev)
            gs = 32 * (n % 2)
            for j in range(8):
                mmi = t.transpose(pTb[:, tb + 2 * j:tb + 2 * (j + 1)],
                                  gstage[gs:gs + 2, 128 * j:128 * (j + 1)],
                                  ident[gs:gs + 2, gs:gs + 2])
            c.inc(mmi, "T", mark=m_tp)

        @op("v")
        def _(v, th_=th_, ec=ec, m_tp=m_tp, m_te=m_te, tb=tb):
            c.wait(v, m_tp)
            gT = gatekT.ap()
            dap = AP(gT.tensor, gT.offset + 128 * th_ + 2 * ec,
                     [[gT.ap[0][0], 128], [16, 8], [1, 2]])
            c.inc(v.tensor_copy(
                dap,
                pTb[:, tb:tb + 16].rearrange("p (j q) -> p j q", j=8)),
                "V", mark=m_te)
        if ec == DC - 1:
            kg_ev[th_] = m_te

    for th_ in range(2):
        for ec in range(DC):
            n = th_ * DC + ec
            base = T * ec + 1024 * th_
            k_base[n] = base
            m_sq, m_ss, m_rs, m_sc = (mk("c7kQ"), mk("c7kS"), mk("c7kR"),
                                      mk("c7kC"))
            k_sq[n] = m_sq; k_ss[n] = m_ss; k_rs[n] = m_rs; k_sc[n] = m_sc
            scfb = scf1 if n % 2 == 0 else kping

            @op("s")
            def _(s, n=n, th_=th_, ec=ec, base=base, m_sq=m_sq):
                c.wait(s, kbev[8 * 2 * th_ + ec])
                c.wait(s, kbev[8 * (2 * th_ + 1) + ec])
                if n == 0:
                    c.wait(s, g0_scb)
                c.wait(s, k_ss.get(n - 1))
                c.inc(s.activation(scb1[:], act3[:, base:base + 1024],
                                   AF.Square), "S", mark=m_sq)

            @op("t")
            def _(t, n=n, m_sq=m_sq, m_ss=m_ss,
                  pfree=(pP1_0 if n == 0 else k_rs[n - 1])):
                c.wait(t, m_sq)
                if n == 0:
                    c.wait(t, "e2:done")
                c.wait(t, pfree)
                for nn in range(2):
                    mmi = t.matmul(pP1[:, 512 * nn:512 * (nn + 1)],
                                   E2rep[:],
                                   scb1[:, 512 * nn:512 * (nn + 1)],
                                   start=True, stop=True)
                c.inc(mmi, "T", mark=m_ss)

            @op("s")
            def _(s, n=n, scfb=scfb, m_ss=m_ss, m_rs=m_rs):
                c.wait(s, m_ss)
                c.wait(s, k_sc.get(n - 2))
                c.inc(s.activation(scfb[:], pP1[:], AF.Sqrt), "S",
                      mark=m_rs)

            @op("v")
            def _(v, base=base, scfb=scfb, m_rs=m_rs, m_sc=m_sc):
                c.wait(v, m_rs)
                v.tensor_scalar(scfb[:], scfb[:], 1e-12, None, ALU.max)
                v.drain()
                v.reciprocal(scfb[:], scfb[:])
                v.drain()
                c.inc(v.tensor_tensor(act3[:, base:base + 1024],
                                      act3[:, base:base + 1024], scfb[:],
                                      ALU.mult), "V", mark=m_sc)

            emit_c3(n)
            if n >= 2:
                k_gate_tp(n - 2)
            if n >= 1:
                k_gate_mm(n - 1)
    NK = 2 * DC
    k_gate_tp(NK - 2)
    k_gate_mm(NK - 1)
    k_gate_tp(NK - 1)
    last["pP1"] = k_rs[NK - 1]
    last["pP2"] = k_te[NK - 1]
    last["pTb"] = k_te[NK - 1]
    last["scb1g"] = k_ss[NK - 1]
    k_sc_last = k_sc[NK - 2]

    def _unused_l2norm(tag, buf, raw_evs, gate_lhs, gate_ready, gate_out,
                       prev_gd=None):
        res = []
        for ec in range(DC):
            for th in range(2):
                idx = ec * 2 + th
                m_sq, m_ss, m_rs, m_sc, m_gm, m_ge = (
                    mk(tag + "Q"), mk(tag + "S"), mk(tag + "R"),
                    mk(tag + "C"), mk(tag + "G"), mk(tag + "H"))
                base = T * ec + 1024 * th
                pfree = last["pP1"]
                last["pP1"] = m_ge

                @op("s")
                def _(s, idx=idx, base=base, m_sq=m_sq,
                      g0=(last["scb1g"] if idx == 0 else None)):
                    c.wait(s, raw_evs[idx])
                    c.wait(s, g0)
                    if idx >= 1:
                        c.wait(s, res[idx - 1][0])
                    c.inc(s.activation(scb1[:], buf[:, base:base + 1024],
                                       AF.Square), "S", mark=m_sq)

                @op("t")
                def _(t, m_sq=m_sq, m_ss=m_ss, pfree=pfree):
                    c.wait(t, m_sq)
                    c.wait(t, "e2:done")
                    c.wait(t, pfree)
                    for nn in range(2):
                        mmi = t.matmul(pP1[:, 512 * nn:512 * (nn + 1)],
                                       E2rep[:],
                                       scb1[:, 512 * nn:512 * (nn + 1)],
                                       start=True, stop=True)
                    c.inc(mmi, "T", mark=m_ss)

                @op("s")
                def _(s, m_ss=m_ss, m_rs=m_rs):
                    c.wait(s, m_ss)
                    c.inc(s.activation(scf1[:], pP1[:], AF.Sqrt), "S",
                          mark=m_rs)

                @op("v")
                def _(v, base=base, m_rs=m_rs, m_sc=m_sc):
                    c.wait(v, m_rs)
                    v.tensor_scalar(scf1[:], scf1[:], 1e-12, None, ALU.max)
                    v.drain()
                    v.reciprocal(scf1[:], scf1[:])
                    v.drain()
                    c.inc(v.tensor_tensor(buf[:, base:base + 1024],
                                          buf[:, base:base + 1024], scf1[:],
                                          ALU.mult), "V", mark=m_sc)

                if gate_out is None:
                    res.append((m_ss, m_sc, m_sc))
                    last["pP1"] = m_sc
                    continue

                @op("t")
                def _(t, base=base, m_sc=m_sc, m_gm=m_gm):
                    c.wait(t, m_sc)
                    c.wait(t, gate_ready)
                    for nn in range(2):
                        mmi = t.matmul(pP1[0:2, 512 * nn:512 * (nn + 1)],
                                       gate_lhs[:, 0:2],
                                       buf[:, base + 512 * nn:
                                           base + 512 * (nn + 1)],
                                       start=True, stop=True)
                    c.inc(mmi, "T", mark=m_gm)

                m_gd = mk(tag + "D")

                @op("v")
                def _(v, ec=ec, th=th, m_gm=m_gm, m_ge=m_ge,
                      prevd=(res[-1][2] if res else prev_gd)):
                    c.wait(v, m_gm)
                    c.wait(v, prevd)
                    c.inc(v.tensor_copy(gstage[0:2, :], pP1[0:2, :]), "V",
                          mark=m_ge)

                @op("g")
                def _(g, ec=ec, th=th, m_ge=m_ge, m_gd=m_gd):
                    c.wait(g, m_ge)
                    c.dma(g, lambda: g.dma_start(
                        out=gate_out[2 * ec:2 * ec + 2,
                                     1024 * th:1024 * (th + 1)],
                        in_=gstage[0:2, :]), "g", mark=m_gd)
                res.append((m_ss, m_ge, m_gd))
        return [r[2] for r in res]

    vp_ev = []
    for tc in range(TC):
        m_vs = mk("c8V")

        @op("v")
        def _(v, tc=tc, m_vs=m_vs):
            c.wait(v, kg_ev[tc // 8])
            c.wait(v, vbev[tc])
            gT = gatekT.ap()
            gb = AP(gT.tensor, gT.offset + 16 * tc,
                    [[gT.ap[0][0], 128], [1, 16], [0, 64]])
            av = act1[:, 1024 * tc:1024 * (tc + 1)].rearrange(
                "p (h q) -> p h q", h=16)
            c.inc(v.tensor_tensor(av, av, gb, ALU.mult), "V", mark=m_vs)
        vp_ev.append(m_vs)
    sfree("gstage")
    sfree("gatekT")

    RB = fmA
    act4 = salloc("act4", [128, HP * 1024], BF16, side="right")
    gqb = Wb0[:, 0:2048]
    stg = Wb0[:, 2048:4096].bitcast(F32)

    qsc = {}
    qn_prev = {"ss": last["scb1g"], "sc": k_sc_last}

    qn_pend = []

    def qnorm_scale():
        while qn_pend:
            idx, base, m_rs, m_sc = qn_pend.pop(0)

            @op("v")
            def _(v, base=base, m_rs=m_rs, m_sc=m_sc):
                c.wait(v, m_rs)
                v.tensor_scalar(scf1[:], scf1[:], 1e-12, None, ALU.max)
                v.drain()
                v.reciprocal(scf1[:], scf1[:])
                v.drain()
                c.inc(v.tensor_tensor(act2[:, base:base + 1024],
                                      act2[:, base:base + 1024], scf1[:],
                                      ALU.mult), "V", mark=m_sc)
            qsc[idx] = m_sc

    def qnorm_inline(idx, split=False):
        ec, thh = idx // 2, idx % 2
        base = T * ec + 1024 * thh
        m_sq, m_ss, m_rs, m_sc = (mk("c7qQ"), mk("c7qS"), mk("c7qR"),
                                  mk("c7qC"))
        ps, pfree = aslot(m_rs)
        pv_ss, pv_sc = qn_prev["ss"], qn_prev["sc"]
        qn_prev["ss"], qn_prev["sc"] = m_ss, m_sc

        @op("v")
        def _(v, base=base, m_sq=m_sq, pv_ss=pv_ss):
            c.wait(v, pv_ss)
            c.inc(v.tensor_tensor(scb1[:], act2[:, base:base + 1024],
                                  act2[:, base:base + 1024], ALU.mult),
                  "V", mark=m_sq)

        @op("t")
        def _(t, ps=ps, m_sq=m_sq, m_ss=m_ss, pfree=pfree):
            c.wait(t, m_sq)
            c.wait(t, "e2:done")
            c.wait(t, pfree)
            for nn in range(2):
                mmi = t.matmul(ps[:, 512 * nn:512 * (nn + 1)], E2rep[:],
                               scb1[:, 512 * nn:512 * (nn + 1)],
                               start=True, stop=True)
            c.inc(mmi, "T", mark=m_ss)

        @op("s")
        def _(s, ps=ps, m_ss=m_ss, m_rs=m_rs, pv_sc=pv_sc):
            c.wait(s, m_ss)
            c.wait(s, pv_sc)
            c.inc(s.activation(scf1[:], ps[:], AF.Sqrt), "S", mark=m_rs)

        qn_pend.append((idx, base, m_rs, m_sc))
        if not split:
            qnorm_scale()

    att_done = []
    prev_cg = {0: None, 1: None}
    c10_st = []
    c10_stm = [mk("cAW") for _ in range(TC)]
    c10_evm = [mk("cAE") for _ in range(TC)]
    lnD_box = [None]
    wo_guard = [None]
    hp_gq_mark = None

    aring = [last["pA"], last["pB"], last["pP1"]]
    apsum = [pA, pB, pP1]
    acnt = [0]

    def aslot(consumer_mark):
        s = acnt[0] % 3
        acnt[0] += 1
        pfree = aring[s]
        aring[s] = consumer_mark
        return apsum[s], pfree

    d_chain = [last["pP2"]]

    def emit_tail(pend):
        h, m_cg1 = pend["h"], pend["cg1"]

        @op("t")
        def _(t, h=h, m_cg1=m_cg1, m_copy=pend["copy"], sgs=pend["sgs"]):
            for m in sgs:
                c.wait(t, m)
            c.wait(t, m_copy)
            c.wait(t, vp_ev[15])
            hb_ = (h % 2) * 64
            for si in range(8):
                sc = 8 + si
                for nn in range(2):
                    mmi = t.matmul(
                        pP2[hb_:hb_ + 64, 512 * nn:512 * (nn + 1)],
                        act1[:, 1024 * sc + 64 * h:1024 * sc + 64 * (h + 1)],
                        RB[:, 1024 * sc + 512 * nn:1024 * sc + 512 * (nn + 1)],
                        start=(si == 0), stop=(si == 7))
            c.inc(mmi, "T", mark=m_cg1)

        @op("v")
        def _(v, h=h, m_cg1=m_cg1, m_ad=pend["ad"], m_ev=pend["ev"],
              m_gq=pend["gq"], wog=pend["wog"]):
            c.wait(v, m_cg1)
            hb_ = (h % 2) * 64
            hp_ = h // 2
            c.inc(v.tensor_tensor(stg[hb_:hb_ + 64, :],
                                  pP2[hb_:hb_ + 64, :],
                                  stg[hb_:hb_ + 64, :], ALU.add), "V",
                  mark=m_ad)
            v.drain()
            c.wait(v, m_gq)
            c.wait(v, wog)
            gslot = 1024 * (hp_ % 2)
            c.inc(v.tensor_tensor(
                act4[hb_:hb_ + 64, 1024 * hp_:1024 * (hp_ + 1)],
                stg[hb_:hb_ + 64, :],
                gqb[hb_:hb_ + 64, gslot:gslot + 1024], ALU.mult),
                "V", mark=m_ev)
        d_chain[0] = pend["ad"]
        att_done.append(pend["ev"])

    pend = None
    for th in range(2):
        qnorm_inline(th)
        pend = None
        for h in range(H):
            hp, hb = h // 2, (h % 2) * 64
            if h % 2 == 0:
                if hp < HP - 1:
                    qnorm_inline(2 * (hp + 1) + th, split=True)
                m_gq, m_gqm = mk("c9GQ"), mk("c9GM")
                ps_g, pfree_g = aslot(m_gq)
                gslot = 1024 * (hp % 2)

                @op("t")
                def _(t, hp=hp, th=th, ps_g=ps_g, m_gqm=m_gqm,
                      pfree_g=pfree_g):
                    c.wait(t, qsc[2 * hp + th])
                    c.wait(t, "gq2rep")
                    c.wait(t, pfree_g)
                    for nn in range(2):
                        mmi = t.matmul(
                            ps_g[:, 512 * nn:512 * (nn + 1)], gq2rep[:],
                            act2[:, T * hp + 1024 * th + 512 * nn:
                                 T * hp + 1024 * th + 512 * (nn + 1)],
                            start=True, stop=True)
                    c.inc(mmi, "T", mark=m_gqm)

                @op("v")
                def _(v, ps_g=ps_g, m_gqm=m_gqm, m_gq=m_gq, gslot=gslot):
                    c.wait(v, m_gqm)
                    c.inc(v.tensor_copy(gqb[:, gslot:gslot + 1024],
                                        ps_g[:]), "V", mark=m_gq)
                hp_gq_mark = m_gq
            else:
                m_gq = hp_gq_mark
                qnorm_scale()

            sg_mm = []

            def emit_align(sc, h=h, th=th):
                m_al, m_sg = mk("c9A"), mk("c9S")
                ps, pfree = aslot(m_sg)
                sg_mm.append(m_sg)
                rbfree = prev_cg[sc // 8]

                @op("t")
                def _(t, h=h, th=th, sc=sc, ps=ps, m_al=m_al, pfree=pfree,
                      first=(h % 2 == 0 and sc == 0)):
                    if first:
                        c.wait(t, qsc[2 * (h // 2) + th])
                    c.wait(t, pfree)
                    hp_, hb_ = h // 2, (h % 2) * 64
                    for nn in range(2):
                        mmi = t.matmul(
                            ps[:, 512 * nn:512 * (nn + 1)],
                            act3[hb_:hb_ + 64,
                                 T * hp_ + 128 * sc:T * hp_ + 128 * (sc + 1)],
                            act2[hb_:hb_ + 64,
                                 T * hp_ + 1024 * th + 512 * nn:
                                 T * hp_ + 1024 * th + 512 * (nn + 1)],
                            start=True, stop=True)
                    c.inc(mmi, "T", mark=m_al)

                @op("s")
                def _(s, sc=sc, ps=ps, m_al=m_al, m_sg=m_sg, rbfree=rbfree):
                    c.wait(s, m_al)
                    c.wait(s, rbfree)
                    c.inc(s.activation(RB[:, 1024 * sc:1024 * (sc + 1)],
                                       ps[:], AF.Sigmoid,
                                       bias=cpool[:, 0:1],
                                       scale=cpool[:, 1:2]), "S",
                          mark=m_sg)

            for sc in range(3):
                emit_align(sc)
            if pend is not None:
                emit_tail(pend)
            for sc in range(3, 11):
                emit_align(sc)

            m_cg0, m_cg1 = mk("c9C0"), mk("c9C1")
            m_copy, m_ad, m_ev = mk("c9P"), mk("c9D"), mk("c9E")
            dfree = d_chain[0]

            @op("t")
            def _(t, h=h, m_cg0=m_cg0, dfree=dfree, sgs=list(sg_mm[0:8])):
                for m in sgs:
                    c.wait(t, m)
                c.wait(t, dfree)
                c.wait(t, vp_ev[7])
                hb_ = (h % 2) * 64
                for si in range(8):
                    for nn in range(2):
                        mmi = t.matmul(
                            pP2[hb_:hb_ + 64, 512 * nn:512 * (nn + 1)],
                            act1[:, 1024 * si + 64 * h:1024 * si + 64 * (h + 1)],
                            RB[:, 1024 * si + 512 * nn:1024 * si + 512 * (nn + 1)],
                            start=(si == 0), stop=(si == 7))
                c.inc(mmi, "T", mark=m_cg0)

            @op("v")
            def _(v, h=h, m_cg0=m_cg0, m_copy=m_copy):
                c.wait(v, m_cg0)
                hb_ = (h % 2) * 64
                c.inc(v.tensor_copy(stg[hb_:hb_ + 64, :],
                                    pP2[hb_:hb_ + 64, :]), "V", mark=m_copy)

            for sc in range(11, 16):
                emit_align(sc)

            prev_cg[0] = m_cg0
            prev_cg[1] = m_cg1
            pend = {"h": h, "cg1": m_cg1, "copy": m_copy, "ad": m_ad,
                    "ev": m_ev, "gq": m_gq, "sgs": list(sg_mm[8:16]),
                    "wog": wo_guard[0]}
        emit_tail(pend)
        pend = None
        last["pA"], last["pB"], last["pP1"] = aring

        if th == 1:
            last["scb1g"] = qn_prev["ss"]
            tp_ping["prev"] = [d_chain[0], d_chain[0]]
            lnD_box[0] = ln_fused(
                "da", dxb0, mv1, inv1, fmA, att_done[-1],
                src_marks=c10_stm, lazy=True,
                stage=[xcb[:, 0:1024], xcb[:, 1024:2048]],
                stage_from=8, stage_marks=c10_evm)
            for _ in range(8):
                lnD_box[0]["emit"]()

        def cw_mms(t, tc, ps, th=th):
            for hp_ in range(HP):
                for nn in range(2):
                    _mmi[0] = t.matmul(
                        ps[:, 512 * nn:512 * (nn + 1)],
                        act4[:, 1024 * hp_ + 128 * (tc % (TC // 2)):
                             1024 * hp_ + 128 * (tc % (TC // 2) + 1)],
                        Wb1[:, 1024 * hp_ + 512 * nn:
                            1024 * hp_ + 512 * (nn + 1)],
                        start=(hp_ == 0), stop=False)
            for nn in range(2):
                _mmi[0] = t.matmul(ps[:, 512 * nn:512 * (nn + 1)],
                                   ones_row[0:1, 0:128],
                                   brow[0:1, 512 * nn:512 * (nn + 1)],
                                   start=False, stop=True)

        prev_half_st = (half_st[-2:] if th == 1 else [])
        half_st = []
        wo_mm_last = None
        for tcl in range(TC // 2):
            tc = th * (TC // 2) + tcl
            ps, pn = pick(tc)
            m_mm, m_ld = mk("cAM"), mk("cAR")
            m_ev = c10_evm[tc]
            m_st = c10_stm[tc]
            pfree = last[pn]
            last[pn] = m_ev
            c10_st.append(m_st)
            half_st.append(m_st)
            wo_mm_last = m_mm

            @op("t")
            def _(t, tc=tc, tcl=tcl, ps=ps, m_mm=m_mm, pfree=pfree, th=th,
                  attw=att_done[-1]):
                if tcl == 0:
                    c.wait(t, attw)
                    c.wait(t, "w:bWo")
                    c.wait(t, "b:bbo")
                c.wait(t, pfree)
                cw_mms(t, tc, ps)
                c.inc(_mmi[0], "T", mark=m_mm)

            @op("y")
            def _(y, tc=tc, tcl=tcl, m_ld=m_ld, th=th,
                  stw=(half_st[tcl - 2] if tcl >= 2 else att_done[-1]),
                  tw=(half_st[tcl - 2] + "t" if (tcl >= 2 and dbg) else None),
                  pw=(prev_half_st[tcl] if (tcl < 2 and prev_half_st)
                      else None)):
                c.wait(y, stw)
                c.wait(y, tw)
                c.wait(y, pw)
                if pw is not None and dbg:
                    c.wait(y, pw + "t")
                c.wait(y, mB_st[tc])
                if dbg:
                    c.wait(y, mB_st[tc] + "t")
                if th == 1 and tcl >= 2:
                    c.wait(y, lnD_box[0]["ap"][tc - 2])
                c.dma(y, lambda: y.dma_start(
                    out=xcb[:, 1024 * (tc % 2):1024 * (tc % 2) + 1024],
                    in_=xrows(dxb1, tc)), "m", mark=m_ld)

            @op("v")
            def _(v, tc=tc, ps=ps, m_mm=m_mm, m_ld=m_ld, m_ev=m_ev):
                c.wait(v, m_mm)
                c.wait(v, m_ld)
                xv = xcb[:, 1024 * (tc % 2):1024 * (tc % 2) + 1024]
                c.inc(v.tensor_tensor(xv, ps[:], xv, ALU.add), "V", mark=m_ev)

            @op("s")
            def _(s, tc=tc, m_ev=m_ev, m_st=m_st):
                c.wait(s, m_ev)
                c.dma(s, lambda: s.dma_start(
                    out=xrows(dxb0, tc),
                    in_=xcb[:, 1024 * (tc % 2):1024 * (tc % 2) + 1024]), "n", mark=m_st)
                if dbg:
                    c.dma(s, lambda: s.dma_start(
                        out=xrows(dtap["t_xc"], tc),
                        in_=xcb[:, 1024 * (tc % 2):1024 * (tc % 2) + 1024]), "n", mark=m_st + "t")
            if th == 1:
                lnD_box[0]["emit"]()
        wo_guard[0] = wo_mm_last
        aring[0] = last["pA"]
        aring[1] = last["pB"]
        aring[2] = last["pP1"]
    mC_st = c10_st[-1]
    last["pP2"] = d_chain[0]
    last["pTb"] = d_chain[0]

    for kk in ("act4", "act1", "act3", "Wb1"):
        sfree(kk)

    lnD = lnD_box[0]
    lnD["flush"]()
    mD_st, _ = ffn_phase("d", dxb0, dy, None, "dO", list(c10_st),
                         att_done[-1], "w:bWo", prev_tap=dbg, lnobj=lnD)

    if c.record:
        pass

    @op("y")
    def _(y):
        for m in mD_st[-2:]:
            c.wait(y, m)

    with nc.Block() as block:
        @block.tensor
        def _(t):
            for fn in work["t"]:
                fn(t)

        @block.vector
        def _(v):
            for fn in work["v"]:
                fn(v)

        @block.scalar
        def _(s):
            for fn in work["s"]:
                fn(s)

        @block.gpsimd
        def _(g):
            for fn in work["g"]:
                fn(g)

        @block.sync
        def _(y):
            for fn in work["y"]:
                fn(y)
            if not c.record:
                for i in range(8):
                    c.wait(y, f"END:dm{i}")
                for i in range(4):
                    c.wait(y, f"END:dn{i}")
                for i in range(80):
                    c.wait(y, f"END:dg{i}")

    if c.record:
        for i in range(8):
            c.tbl[f"END:dm{i}"] = (f"dm{i}", c.cnt[f"dm{i}"])
        for i in range(4):
            c.tbl[f"END:dn{i}"] = (f"dn{i}", c.cnt[f"dn{i}"])
        for i in range(80):
            c.tbl[f"END:dg{i}"] = (f"dg{i}", c.cnt[f"dg{i}"])


def build_core(TC=16, dbg=False):
    tbl = {}
    nc = None
    for pass_i in (0, 1):
        nc = bass.Bass("TRN2", target_bir_lowering=False)
        c = Ctx(nc, record=(pass_i == 0), tbl=tbl)
        _emit(nc, c, TC, dbg)
    return nc


_IN_MAP = [
    ("x", "L0_attn_a_x"),
    ("aWq", "L0_attn_a_Wq"), ("aWk", "L0_attn_a_Wk"), ("aWv", "L0_attn_a_Wv"),
    ("abq", "L0_attn_a_bq"), ("abk", "L0_attn_a_bk"), ("abv", "L0_attn_a_bv"),
    ("bWq", "L0_attn_b_Wq"), ("bWk", "L0_attn_b_Wk"), ("bWv", "L0_attn_b_Wv"),
    ("bWo", "L0_attn_b_Wo"), ("bgq", "L0_attn_b_gq"), ("bgk", "L0_attn_b_gk"),
    ("bbo", "L0_attn_b_bo"), ("blnw", "L0_attn_b_ln_w"),
    ("blnb", "L0_attn_b_ln_b"),
    ("fWg", "L0_ffn_Wg"), ("fA", "L0_ffn_A"), ("fB", "L0_ffn_B"),
]


def kernel(**inputs):
    B = inputs["L0_attn_a_x"].shape[0]
    nc = build_core(TC=inputs["L0_attn_a_x"].shape[1] // 128, dbg=False)
    in_maps = []
    for b in range(B):
        m = {}
        for knm, inm in _IN_MAP:
            a = np.ascontiguousarray(np.asarray(inputs[inm], dtype=np.float32))
            if knm == "x":
                a = np.ascontiguousarray(a[b])
            m[knm] = a
        in_maps.append(m)
    res = run_bass_kernel_spmd(nc, in_maps, core_ids=list(range(B)))
    return np.stack([r["y"] for r in res.results], axis=0)
